# version 07
# baseline (speedup 1.0000x reference)
import numpy as np
import ml_dtypes
from contextlib import ExitStack
import concourse.bass as bass
import concourse.mybir as mybir
from concourse.bass_utils import run_bass_kernel_spmd

F32 = mybir.dt.float32
BF16 = mybir.dt.bfloat16
AF = mybir.ActivationFunctionType
ALU = mybir.AluOpType

D = 2048
KD = D // 128
DIN = 4096
CONVD = 4352
NCT = CONVD // 128
CW = 515
PROJ = 8512
NH = 64
INTER = 5632
NJT = INTER // 128
TP = 512
NT = TP // 128
EPS = 1e-5
CCH = 2128

N_PASS = 8
STAGE = 6
TAPS = ()


def build_program():
    nc = bass.Bass("TRN2", target_bir_lowering=False, detect_race_conditions=False)

    x_in = nc.dram_tensor("x", [2, 2048, D], F32, kind="ExternalInput").ap()
    winT = nc.dram_tensor("winT", [D, PROJ], F32, kind="ExternalInput").ap()
    woutT = nc.dram_tensor("woutT", [DIN, D], F32, kind="ExternalInput").ap()
    wgT = nc.dram_tensor("wgT", [D, INTER], F32, kind="ExternalInput").ap()
    wuT = nc.dram_tensor("wuT", [D, INTER], F32, kind="ExternalInput").ap()
    wdT = nc.dram_tensor("wdT", [INTER, D], F32, kind="ExternalInput").ap()
    convw_in = nc.dram_tensor("convw", [128, NCT * 4], F32, kind="ExternalInput").ap()
    convb_in = nc.dram_tensor("convb", [128, NCT], F32, kind="ExternalInput").ap()
    dtb_in = nc.dram_tensor("dtb", [NH, 1], F32, kind="ExternalInput").ap()
    alog_in = nc.dram_tensor("alog", [NH, 1], F32, kind="ExternalInput").ap()
    dh_rep_in = nc.dram_tensor("dh_rep", [128, NH], F32, kind="ExternalInput").ap()
    nw_rep_in = nc.dram_tensor("nw_rep", [128, DIN], F32, kind="ExternalInput").ap()
    gs_in = nc.dram_tensor("gs", [128, NJT], F32, kind="ExternalInput").ap()
    us_in = nc.dram_tensor("us", [128, NJT], F32, kind="ExternalInput").ap()
    dsr_in = nc.dram_tensor("ds_rep", [128, D], F32, kind="ExternalInput").ap()
    out = nc.dram_tensor("out", [2, 2048, D], F32, kind="ExternalOutput").ap()

    winT16 = nc.dram_tensor("winT16", [D, PROJ], BF16, kind="Internal").ap()
    woutT16 = nc.dram_tensor("woutT16", [DIN, D], BF16, kind="Internal").ap()
    wgT16 = nc.dram_tensor("wgT16", [D, INTER], BF16, kind="Internal").ap()
    wuT16 = nc.dram_tensor("wuT16", [D, INTER], BF16, kind="Internal").ap()
    wdT16 = nc.dram_tensor("wdT16", [INTER, D], BF16, kind="Internal").ap()
    gate_sp = nc.dram_tensor("gate_sp", [TP, DIN], BF16, kind="Internal").ap()
    ynT_sp = nc.dram_tensor("ynT_sp", [DIN, TP], BF16, kind="Internal").ap()

    tap_specs = {
        "t_xn1T": ([128, KD * TP], BF16),
        "t_gate": ([TP, DIN], BF16),
        "t_convin": ([128, NCT * CW], BF16),
        "t_dtraw": ([NH, TP], F32),
        "t_dtsp": ([NH, TP], F32),
        "t_convout": ([128, NCT * CW], BF16),
        "t_acs": ([NH, 128], F32),
        "t_xmtok": ([128, DIN], BF16),
        "t_y": ([128, DIN], BF16),
        "t_yn": ([128, DIN], BF16),
        "t_ynT": ([DIN, TP], BF16),
        "t_attn": ([128, NT * D], F32),
        "t_x1": ([128, NT * D], F32),
        "t_h": ([128, 32 * TP], BF16),
        "t_dbg_sg": ([128, TP], BF16),
        "t_dbg_rms": ([128, 64], F32),
        "t_dbg_g": ([128, TP], F32),
        "t_dbg_u": ([128, TP], F32),
        "t_S": ([128, DIN], F32),
    }
    taps = {}
    for name in TAPS:
        shape, dt = tap_specs[name]
        taps[name] = nc.dram_tensor(name, shape, dt, kind="ExternalOutput").ap()

    es = ExitStack()
    sbuf = lambda n, s, d: es.enter_context(nc.sbuf_tensor(n, s, d))
    psal = lambda n, s, d: es.enter_context(nc.psum_tensor(n, s, d))

    with es:
        x_res = sbuf("x_res", [128, NT * D], F32)
        xn_tok_b = sbuf("xn_tok", [128, 2 * D], BF16)
        xn_tok = [xn_tok_b[:, i * D:(i + 1) * D] for i in range(2)]
        ynT_ch = xn_tok_b[:, 0:DIN]
        xnT_b = sbuf("xnT", [128, KD * TP], BF16)
        xnT = xnT_b[:, :]
        flat = xnT_b[0:2, :]
        convin_b = sbuf("convin", [128, NCT * CW], BF16)
        convin = convin_b[:, :]
        tail = sbuf("tail", [128, NCT * 3], BF16)
        dt_raw = sbuf("dt_raw", [NH, TP], F32)
        dt_sp = sbuf("dt_sp", [NH, TP], F32)
        sp_scr2 = sbuf("sp_scr2", [NH, TP], F32)
        a_dt = sp_scr2[:, :]
        acsT = sbuf("acsT", [NH, 128], F32)
        hilo = sbuf("hilo", [NH, 256], BF16)
        wdecT = sbuf("wdecT", [NH, 128], F32)
        dtwT = sbuf("dtwT", [NH, 128], F32)
        stok = sbuf("stok", [128, 192], F32)
        expa = sbuf("expa", [128, NH], F32)
        decr = sbuf("decr", [128, NH], F32)
        b_tok = sbuf("b_tok", [128, 128], BF16)
        gm = sbuf("gm", [128, 128], BF16)
        s_state = sbuf("s_state", [128, DIN], F32)

        A1 = sbuf("A1", [128, 16384], BF16)
        x_m = A1[:, 0:4096]
        y_tok = A1[:, 4096:8192]
        dnegb = [A1[:, 8192 + i * 2048: 8192 + (i + 1) * 2048].bitcast(F32) for i in range(2)]
        e_sb = [A1[:, 12288 + i * 1024: 12288 + (i + 1) * 1024] for i in range(2)]
        sdec = [A1[:, 14336 + i * 1024: 14336 + (i + 1) * 1024].bitcast(F32) for i in range(2)]
        attn_tok = A1[:, :].bitcast(F32)

        A2 = sbuf("A2", [128, 4608], BF16)
        mt = [A2[:, i * 128:(i + 1) * 128] for i in range(2)]
        dres = [A2[:, 256 + i * 512: 256 + (i + 1) * 512] for i in range(2)]
        xcw = [A2[:, 1280 + i * 512: 1280 + (i + 1) * 512] for i in range(2)]
        tmpg = A2[:, 2304:3328].bitcast(F32)
        s16g = [A2[:, 3328 + i * 512: 3328 + (i + 1) * 512] for i in range(2)]
        zerb = A2[:, 4352:4608].bitcast(F32)
        zer = zerb[0:NH, :]
        gstag = [A2[:, i * 512:(i + 1) * 512] for i in range(2)]
        diagw = [A2[:, 2048 + i * 128: 2048 + (i + 1) * 128] for i in range(8)]
        it1 = A2[:, 0:256].bitcast(F32)
        it2 = A2[:, 256:512].bitcast(F32)
        ynstr = [A2[:, i * 512:(i + 1) * 512] for i in range(4)]
        aostr = [A2[:, 2048 + i * 1024: 2048 + (i + 1) * 1024].bitcast(F32) for i in range(2)]
        sg_j = [A2[:, i * 512:(i + 1) * 512] for i in range(2)]

        gate_ch_b = sbuf("gate_ch", [128, 2 * DIN], BF16)
        gate_ch = [gate_ch_b[:, i * DIN:(i + 1) * DIN] for i in range(2)]

        def hT(j):
            if j < 32:
                return convin_b[:, j * TP:(j + 1) * TP]
            return gate_ch_b[:, (j - 32) * TP:(j - 31) * TP]

        bn6 = sbuf("bn6", [128, 96], F32)
        mv = sbuf("mv", [128, 2 * NT], F32)
        spacr = sbuf("spacr", [128, 32], F32)
        rstd = sbuf("rstd", [128, NT], F32)
        vtmp = sbuf("vtmp", [128, NT], F32)
        epsc = sbuf("epsc", [128, 1], F32)
        tiny = sbuf("tiny", [128, 8], F32)
        warena_b = sbuf("warena", [128, 3 * 2048], BF16)
        warena = [warena_b[:, i * 2048:(i + 1) * 2048] for i in range(3)]
        wcast_b = sbuf("wcast", [128, 2 * CCH], BF16)
        wcast = [wcast_b[:, i * CCH:(i + 1) * CCH] for i in range(2)]
        identb = sbuf("identb", [128, 128], BF16)
        identf = sbuf("identf", [128, 128], F32)
        triu01 = sbuf("triu01", [128, 128], BF16)
        ones2 = sbuf("ones2", [2, 128], BF16)
        convw = sbuf("convw_s", [128, NCT * 4], F32)
        convb = sbuf("convb_s", [128, NCT], F32)
        dtb = sbuf("dtb_s", [NH, 1], F32)
        acol = sbuf("acol", [NH, 1], F32)
        dh_rep = sbuf("dh_rep_s", [128, NH], F32)
        nw_rep = sbuf("nw_rep_s", [128, DIN], BF16)
        gs_c = sbuf("gs_c", [128, NJT], F32)
        us_c = sbuf("us_c", [128, NJT], F32)
        ds_rep = xnT_b[:, :].bitcast(F32)[:, 0:D]

        pA = psal("pA", [128, 1024], F32)
        p1 = psal("p1", [128, 512], F32)
        p2 = psal("p2", [128, 512], F32)
        p3 = psal("p3", [128, 512], F32)
        p4 = psal("p4", [128, 512], F32)
        p5 = psal("p5", [128, 512], F32)
        pb16 = psal("pb16", [128, 1024], BF16)
        pt4 = [p1, p2, p3, p4]
        pxb = [pb16[:, 0:512], p5[:, :].bitcast(BF16)[:, 0:512]]

        class Sem:
            def __init__(self, name):
                self.h = es.enter_context(nc.semaphore(name))
                self.n = 0

            def inc(self, instr, k=1):
                instr.then_inc(self.h, k)
                self.n += k
                return self.n

        s_ld = Sem("s_ld")
        s_w3 = [Sem(f"s_w{i}") for i in range(3)]
        s_y4 = [Sem(f"s_y{i}") for i in range(4)]
        s_gs2 = [Sem(f"s_gs{i}") for i in range(2)]
        s_gpre = Sem("s_gpre")
        s_ynsp = Sem("s_ynsp")
        s_dsr = Sem("s_dsr")
        s_sp = Sem("s_sp")
        s_cl2 = [Sem(f"s_cl{i}") for i in range(2)]
        s_cw2 = [Sem(f"s_cw{i}") for i in range(2)]
        s_pe = Sem("s_pe")
        s_ve = Sem("s_ve")
        s_ac = Sem("s_ac")
        s_gp = Sem("s_gp")
        s_out = Sem("s_out")

        block = es.enter_context(nc.Block())

        def r3(ap2d, inner):
            return ap2d.rearrange("p (a b) -> p a b", b=inner)

        def wtile3(wT, row0, nkt, col0, wid):
            v = wT.rearrange("(a p) j -> p a j", p=128)
            return v[:, row0 // 128: row0 // 128 + nkt, col0:col0 + wid]

        @block.sync
        def _(sync):
            for dst, src in [
                (convw[:, :], convw_in), (convb[:, :], convb_in),
                (dtb[:, :], dtb_in), (acol[:, :], alog_in),
                (dh_rep[:, :], dh_rep_in),
                (gs_c[:, :], gs_in), (us_c[:, :], us_in),
            ]:
                s_ld.inc(sync.dma_start(out=dst, in_=src), 16)
        consts_ld = s_ld.n

        @block.gpsimd
        def _(g):
            s_sp.inc(g.dma_start(out=nw_rep[:, :], in_=nw_rep_in), 16)
            g.iota(it1[:, :], pattern=[[1, 128]], base=0, channel_multiplier=0,
                   allow_small_or_imprecise_dtypes=True)
            s_gp.inc(g.iota(it2[:, :], pattern=[[0, 128]], base=0, channel_multiplier=1,
                            allow_small_or_imprecise_dtypes=True))

        @block.vector
        def _(v):
            v.wait_ge(s_gp.h, s_gp.n)
            v.tensor_tensor(identf[:, :], it1[:, :], it2[:, :], ALU.is_equal)
            v.tensor_copy(identb[:, :], identf[:, :])
            v.tensor_tensor(triu01[:, :], it1[:, :], it2[:, :], ALU.is_ge)
            v.memset(ones2[:, :], 1.0)
            v.memset(epsc[:, :], EPS)
            v.memset(s_state[:, :], 0.0)
            v.memset(convin_b[:, :], 0.0)
            s_ve.inc(v.memset(tail[:, :], 0.0))

        @block.scalar
        def _(sc):
            sc.wait_ge(s_ld.h, consts_ld)
            s_ac.inc(sc.activation(acol[:, :], acol[:, :], AF.Exp))

        @block.vector
        def _(v):
            v.wait_ge(s_ac.h, s_ac.n)
            s_ve.inc(v.tensor_scalar(acol[:, :], acol[:, :], -1.0, None, ALU.mult))

        def cast_jobs_for(ws):
            jobs = []
            for wf, wb, rows, cols in ws:
                for rr in range(0, rows, 128):
                    for c0 in range(0, cols, CCH):
                        jobs.append((wf, wb, rr, c0, min(CCH, cols - c0)))
            return jobs

        wready = {}
        cast_state = {"i": 0}

        def emit_cast(jobs):
            i0 = cast_state["i"]
            @block.gpsimd
            def _(g):
                for k, (wf, wb, rr, c0, cw_) in enumerate(jobs):
                    i = i0 + k
                    sl = i % 2
                    if i >= 2:
                        g.wait_ge(s_cw2[sl].h, s_cw2[sl].n)
                    s_cl2[sl].inc(
                        g.dma_start(out=wcast[sl][:, 0:cw_], in_=wf[rr:rr + 128, c0:c0 + cw_]), 16)
                    g.wait_ge(s_cl2[sl].h, s_cl2[sl].n)
                    s_cw2[sl].inc(g.dma_start(out=wb[rr:rr + 128, c0:c0 + cw_],
                                              in_=wcast[sl][:, 0:cw_]), 16)
                    wready[wb.tensor.name] = (s_cw2[0].n, s_cw2[1].n)
            cast_state["i"] = i0 + len(jobs)

        def wait_cast(eng, name):
            c0, c1 = wready[name]
            eng.wait_ge(s_cw2[0].h, c0)
            eng.wait_ge(s_cw2[1].h, c1)

        emit_cast(cast_jobs_for([(winT, winT16, D, PROJ), (woutT, woutT16, DIN, D)]))
        late_cast = cast_jobs_for([(wgT, wgT16, D, INTER), (wuT, wuT16, D, INTER),
                                   (wdT, wdT16, INTER, D)])

        def tap(name, src_ap):
            if name not in taps:
                return
            @block.sync
            def _(sync):
                sync.wait_ge(s_ve.h, s_ve.n)
                sync.wait_ge(s_ac.h, s_ac.n)
                sync.wait_ge(s_pe.h, s_pe.n)
                sync.wait_ge(s_gp.h, s_gp.n)
                sync.wait_ge(s_ld.h, s_ld.n)
                sync.wait_ge(s_sp.h, s_sp.n)
                s_out.inc(sync.dma_start(out=taps[name], in_=src_ap), 16)
            @block.vector
            def _(v):
                v.wait_ge(s_out.h, s_out.n)

        def ln_stats(src):
            nsub = D // 512
            @block.vector
            def _(v):
                for s in range(nsub):
                    for i in range(NT):
                        v.bn_stats(bn6[:, (i * nsub + s) * 6:(i * nsub + s + 1) * 6],
                                   src[:, i * D + s * 512:i * D + (s + 1) * 512])
                for i in range(NT):
                    v.bn_aggr(mv[:, 2 * i:2 * i + 2], bn6[:, i * nsub * 6:(i + 1) * nsub * 6])
                v.tensor_copy(spacr[:, 0:32], identf[:, 0:32])
                v.tensor_copy(spacr[:, 0:32], identf[:, 32:64])
                s_ve.inc(v.tensor_copy(spacr[:, 0:32], identf[:, 64:96]))

            @block.scalar
            def _(sc):
                sc.wait_ge(s_ve.h, s_ve.n)
                mvv = r3(mv[:, :], 2)[:, :, 1:2]
                sc.activation(r3(rstd[:, 0:NT], 1), mvv, AF.Ln, bias=epsc[:, :])
                s_ac.inc(sc.activation(rstd[:, 0:NT], rstd[:, 0:NT], AF.Exp, scale=-0.5))

        def ln_apply_transpose(src):
            stats_ac = s_ac.n
            pe_b = s_pe.n
            ve_b = s_ve.n

            def evac_mark(gr):
                return ve_b + 5 * (gr // 4) + 2 + (gr % 4)

            for i in range(NT):
                @block.vector
                def _(v):
                    v.wait_ge(s_ac.h, stats_ac)
                    if i >= 2:
                        v.wait_ge(s_pe.h, pe_b + 4 * (i - 1))
                    s_ve.inc(v.tensor_scalar(xn_tok[i % 2], src[:, i * D:(i + 1) * D],
                                             mv[:, 2 * i:2 * i + 1], rstd[:, i:i + 1],
                                             ALU.subtract, ALU.mult))

                @block.tensor
                def _(t):
                    t.wait_ge(s_ve.h, ve_b + 5 * i + 1)
                    for r in range(4):
                        gr = 4 * i + r
                        if gr >= 2:
                            t.wait_ge(s_ve.h, evac_mark(gr - 2))
                        dstp = pxb[gr % 2]
                        ti = None
                        for kk in range(4):
                            k = r * 4 + kk
                            ti = t.transpose(dstp[:, kk * 128:(kk + 1) * 128],
                                             xn_tok[i % 2][:, k * 128:(k + 1) * 128], identb[:, :])
                        s_pe.inc(ti)

                @block.vector
                def _(v):
                    for r in range(4):
                        gr = 4 * i + r
                        v.wait_ge(s_pe.h, pe_b + gr + 1)
                        dst = r3(xnT, TP)[:, r * 4:r * 4 + 4, i * 128:(i + 1) * 128]
                        s_ve.inc(v.tensor_copy(dst, r3(pxb[gr % 2], 128)))
                        assert s_ve.n == evac_mark(gr)
            assert s_pe.n == pe_b + 4 * NT and s_ve.n == ve_b + 5 * NT

        prev_flat_pe = 0
        prev_prep_ac = 0
        prev_ynT_ve = 0
        last_yn_pe = {}
        for p in range(N_PASS):
            bi = p // 4
            r0 = (p % 4) * TP
            first = (p % 4) == 0

            spill_g = {}
            gstag_g = {}
            evac_g = {}
            evac_x = {}
            diag_m = {}
            convev = {}
            gate_pref = {}
            ynsp = {}
            dneg_m = {}
            e_m = {}
            mt_cons = {}
            mt_cons_ve = {}
            yevac = {}
            supd = {}
            tr_m = {}
            h_m = {}
            sg_m = {}
            dv_m = {}
            dtr_m = {}
            s16_m = {}
            s16_cons = {}

            @block.sync
            def _(sync):
                if p > 0:
                    sync.wait_ge(s_out.h, s_out.n)
                for i in range(NT):
                    s_ld.inc(sync.dma_start(out=x_res[:, i * D:(i + 1) * D],
                                            in_=x_in[bi, r0 + i * 128:r0 + (i + 1) * 128, :]), 16)
            xload = s_ld.n

            @block.vector
            def _(v):
                v.wait_ge(s_ld.h, xload)
            ln_stats(x_res)
            @block.tensor
            def _(t):
                t.wait_ge(s_ve.h, s_ve.n)
            ln_apply_transpose(x_res)
            xnT_ve = s_ve.n
            tap("t_xn1T", xnT)

            pe_b_g = s_pe.n
            wmark_g = {}

            def load_gate_quarter(sc, idx):
                jc, hh = idx // 4, idx % 4
                sl = idx % 3
                if idx >= 3:
                    sc.wait_ge(s_pe.h, pe_b_g + idx - 2)
                src = wtile3(winT16, hh * 512, 4, jc * 512, 512)
                wmark_g[idx] = (sl, s_w3[sl].inc(sc.dma_start(out=r3(warena[sl], 512), in_=src), 16))

            @block.scalar
            def _(sc):
                wait_cast(sc, "winT16")
                for idx in range(3):
                    load_gate_quarter(sc, idx)

            for jc in range(8):
                @block.scalar
                def _(sc):
                    for hh in range(4):
                        idx = jc * 4 + hh + 3
                        if idx < 32:
                            load_gate_quarter(sc, idx)

                @block.tensor
                def _(t):
                    if jc == 0:
                        t.wait_ge(s_ve.h, xnT_ve)
                    if jc >= 1:
                        t.wait_ge(s_ac.h, evac_g[jc - 1])
                    for hh in range(4):
                        idx = jc * 4 + hh
                        sl, wv = wmark_g[idx]
                        t.wait_ge(s_w3[sl].h, wv)
                        ti = None
                        for tt in range(NT):
                            for kk in range(4):
                                k = hh * 4 + kk
                                ti = t.matmul(pt4[tt][:, :],
                                              xnT[:, k * TP + tt * 128:k * TP + (tt + 1) * 128],
                                              warena[sl][:, kk * 512:(kk + 1) * 512],
                                              start=(k == 0), stop=(k == 15),
                                              skip_group_check=True)
                        s_pe.inc(ti)
                mm_g = s_pe.n

                for tt in range(NT):
                    u = jc * NT + tt
                    sl = tt % 2
                    @block.scalar
                    def _(sc):
                        if tt == 0:
                            sc.wait_ge(s_pe.h, mm_g)
                        if u >= 2:
                            sc.wait_ge(s_gs2[sl].h, spill_g[u - 2])
                        gstag_g[u] = s_ac.inc(sc.activation(gstag[sl], pt4[tt][:, :], AF.Silu))

                    @block.sync
                    def _(sync):
                        sync.wait_ge(s_ac.h, gstag_g[u])
                        spill_g[u] = s_gs2[sl].inc(
                            sync.dma_start(out=gate_sp[tt * 128:(tt + 1) * 128, jc * 512:(jc + 1) * 512],
                                           in_=gstag[sl]), 16)
                evac_g[jc] = s_ac.n
            gate_spilled = (s_gs2[0].n, s_gs2[1].n)
            tap("t_gate", gate_sp)

            pe_b_x = s_pe.n
            wmark_x = {}
            @block.scalar
            def _(sc):
                sc.wait_ge(s_pe.h, pe_b_x)
                for jt in range(35):
                    sl = jt % 3
                    if jt >= 3:
                        sc.wait_ge(s_pe.h, pe_b_x + jt - 2)
                    wid = 128 if jt < 34 else 64
                    src = wtile3(winT16, 0, KD, 4096 + jt * 128, wid)
                    wmark_x[jt] = (sl, s_w3[sl].inc(
                        sc.dma_start(out=r3(warena[sl][:, 0:KD * wid], wid), in_=src), 16))

            for jt in range(35):
                wid = 128 if jt < 34 else 64
                dstp = (pA[:, 0:512] if jt % 2 == 0 else pA[:, 512:1024]) if jt < 34 else p5[:, :]
                @block.tensor
                def _(t):
                    sl, wv = wmark_x[jt]
                    t.wait_ge(s_w3[sl].h, wv)
                    if jt >= 2:
                        t.wait_ge(s_ve.h, evac_x[jt - 2])
                    ti = None
                    for k in range(KD):
                        o = dstp[0:wid, :] if wid == 64 else dstp
                        ti = t.matmul(o, warena[sl][:, k * wid:(k + 1) * wid],
                                      xnT[:, k * TP:(k + 1) * TP],
                                      start=(k == 0), stop=(k == KD - 1))
                    s_pe.inc(ti)
                mm_x = s_pe.n

                @block.vector
                def _(v):
                    v.wait_ge(s_pe.h, mm_x)
                    if jt < 34:
                        evac_x[jt] = s_ve.inc(
                            v.tensor_copy(convin[:, jt * CW + 3:jt * CW + 3 + 512], dstp))
                    else:
                        evac_x[jt] = s_ve.inc(v.tensor_copy(dt_raw[:, :], dstp[0:NH, :]))
            inproj_ve = s_ve.n
            tap("t_dtraw", dt_raw[:, :])
            tap("t_convin", convin)

            if STAGE < 2:
                _store(block, (s_ve,), s_out, x_res, out, bi, r0)
                continue

            tailv = r3(convin, CW)[:, :, 512:515]
            halov = r3(convin, CW)[:, :, 0:3]
            @block.vector
            def _(v):
                if first:
                    v.memset(halov, 0.0)
                else:
                    v.tensor_copy(halov, r3(tail[:, :], 3))
                s_ve.inc(v.tensor_copy(r3(tail[:, :], 3), tailv))
            conv_rdy = s_ve.n

            pe_b_c = s_pe.n
            @block.vector
            def _(v):
                for j in range(NCT):
                    for k in range(4):
                        idx = j * 4 + k
                        if j >= 2 and k == 0:
                            v.wait_ge(s_pe.h, pe_b_c + j - 1)
                        diag_m[idx] = s_ve.inc(
                            v.tensor_scalar(diagw[idx % 8], identb[:, :],
                                            convw[:, j * 4 + k:j * 4 + k + 1], None, ALU.mult))

            for j in range(NCT):
                @block.tensor
                def _(t):
                    if j == 0:
                        t.wait_ge(s_ve.h, conv_rdy)
                    if j >= 2:
                        t.wait_ge(s_ac.h, convev[j - 2])
                    ti = None
                    for k in range(4):
                        idx = j * 4 + k
                        t.wait_ge(s_ve.h, diag_m[idx])
                        ti = t.matmul(pt4[j % 2][:, :], diagw[idx % 8],
                                      convin[:, j * CW + k:j * CW + k + 512],
                                      start=(k == 0), stop=(k == 3))
                    s_pe.inc(ti)
                mm_c = s_pe.n

                @block.scalar
                def _(sc):
                    sc.wait_ge(s_pe.h, mm_c)
                    convev[j] = s_ac.inc(
                        sc.activation(convin[:, j * CW + 3:j * CW + 3 + 512],
                                      pt4[j % 2][:, :], AF.Silu,
                                      bias=convb[:, j:j + 1], scale=1.0))
            conv_ac = s_ac.n
            tap("t_convout", convin)

            if STAGE < 3:
                _store(block, (s_ve, s_ac), s_out, x_res, out, bi, r0)
                continue

            @block.vector
            def _(v):
                if "t_dtraw" in taps or "t_convin" in taps or "t_gate" in taps:
                    v.wait_ge(s_out.h, s_out.n)
                s_ve.inc(v.tensor_scalar(dt_raw[:, :], dt_raw[:, :], dtb[:, :], None, ALU.add))
            m1 = s_ve.n
            @block.scalar
            def _(sc):
                sc.wait_ge(s_ve.h, m1)
                sc.activation(sp_scr2[:, :], dt_raw[:, :], AF.Abs)
                sc.activation(sp_scr2[:, :], sp_scr2[:, :], AF.Exp, scale=-1.0)
                s_ac.inc(sc.activation(sp_scr2[:, :], sp_scr2[:, :], AF.Ln, bias=1.0))
            m2 = s_ac.n
            @block.vector
            def _(v):
                v.wait_ge(s_ac.h, m2)
                v.tensor_scalar(dt_raw[:, :], dt_raw[:, :], 0.0, None, ALU.max)
                v.tensor_tensor(dt_sp[:, :], dt_raw[:, :], sp_scr2[:, :], ALU.add)
                v.memset(zer[:, :], 0.0)
                s_ve.inc(v.tensor_scalar(a_dt[:, :], dt_sp[:, :], acol[:, :], None, ALU.mult))
            adt_ve = s_ve.n
            tap("t_dtsp", dt_sp[:, :])

            @block.sync
            def _(sync):
                sync.wait_ge(s_gs2[0].h, gate_spilled[0])
                sync.wait_ge(s_gs2[1].h, gate_spilled[1])
                sync.wait_ge(s_pe.h, last_yn_pe.get(0, 0))
                gate_pref[0] = s_gpre.inc(sync.dma_start(out=gate_ch[0], in_=gate_sp[0:128, :]), 16)

            for c in range(4):
                cb = 3 + c * 128

                @block.vector
                def _(v):
                    if c == 0:
                        v.wait_ge(s_ve.h, adt_ve)
                        if first:
                            v.memset(s_state[:, :], 0.0)
                    else:
                        v.wait_ge(s_sp.h, s_sp.n)
                    v.tensor_tensor_scan(acsT[:, :], a_dt[:, c * 128:(c + 1) * 128],
                                         zer, 0.0, ALU.add, ALU.add)
                    v.tensor_copy(hilo[:, 0:128], acsT[:, :])
                    v.tensor_tensor(hilo[:, 128:256], acsT[:, :], hilo[:, 0:128], ALU.subtract)
                    s_ve.inc(v.tensor_scalar(wdecT[:, :], acsT[:, :], acsT[:, 127:128], -1.0,
                                             ALU.subtract, ALU.mult))
                ach1 = s_ve.n
                @block.scalar
                def _(sc):
                    sc.wait_ge(s_ve.h, ach1)
                    s_ac.inc(sc.activation(wdecT[:, :], wdecT[:, :], AF.Exp))
                wdec_ac = s_ac.n
                @block.vector
                def _(v):
                    v.wait_ge(s_ac.h, wdec_ac)
                    s_ve.inc(v.tensor_tensor(dtwT[:, :], wdecT[:, :],
                                             dt_sp[:, c * 128:(c + 1) * 128], ALU.mult))
                dtw_ve = s_ve.n

                @block.gpsimd
                def _(g):
                    g.wait_ge(s_ve.h, ach1)
                    g.wait_ge(s_pe.h, prev_flat_pe)
                    s_sp.inc(g.dma_start(out=flat[0:1, :], in_=hilo[:, 0:128]), 16)
                    s_sp.inc(g.dma_start(out=flat[1:2, :], in_=hilo[:, 128:256]), 16)
                flat_sp = s_sp.n

                @block.tensor
                def _(t):
                    if c == 0:
                        t.wait_ge(s_ac.h, conv_ac)
                    t.wait_ge(s_ve.h, dtw_ve)
                    t.wait_ge(s_ac.h, prev_prep_ac)
                    t.wait_ge(s_ve.h, prev_ynT_ve)
                    t.transpose(p5[:, 0:64], dt_sp[:, c * 128:(c + 1) * 128], identf[0:64, 0:64])
                    t.transpose(p5[:, 64:128], acsT[:, :], identf[0:64, 0:64])
                    t.transpose(p5[:, 128:192], dtwT[:, :], identf[0:64, 0:64])
                    t.matmul(p5[:, 192:320], convin[:, 32 * CW + cb:32 * CW + cb + 128],
                             convin[:, 33 * CW + cb:33 * CW + cb + 128], start=True, stop=True)
                    t.wait_ge(s_sp.h, flat_sp)
                    fv = flat.rearrange("c (h t) -> c h t", h=NH)[:, :, 127]
                    t.matmul(p5[:, 320:384], ones2[:, :], fv, start=True, stop=True)
                    s_pe.inc(t.transpose(pb16[:, 0:128],
                                         convin[:, 32 * CW + cb:32 * CW + cb + 128], identb[:, :]))
                prep_pe = s_pe.n

                @block.vector
                def _(v):
                    v.wait_ge(s_pe.h, prep_pe)
                    v.tensor_copy(stok[:, :], p5[:, 0:192])
                    v.tensor_tensor(gm[:, :], p5[:, 192:320], triu01[:, :], ALU.mult)
                    s_ve.inc(v.tensor_copy(b_tok[:, :], pb16[:, 0:128]))
                prep_ve = s_ve.n

                @block.scalar
                def _(sc):
                    sc.wait_ge(s_ve.h, prep_ve)
                    sc.activation(expa[:, :], stok[:, 64:128], AF.Exp)
                    s_ac.inc(sc.activation(decr[:, :], p5[:, 320:384], AF.Exp))
                prep_ac = s_ac.n
                prev_prep_ac = prep_ac

                pe_b_xm = s_pe.n
                ve_b_xm = s_ve.n
                @block.tensor
                def _(t):
                    t.wait_ge(s_ve.h, prep_ve)
                    t.wait_ge(s_ac.h, prep_ac)
                    for r in range(8):
                        if r >= 2:
                            t.wait_ge(s_ve.h, ve_b_xm + r - 1)
                        dstp = pxb[r % 2]
                        ti = None
                        for kk in range(4):
                            j = r * 4 + kk
                            ti = t.transpose(dstp[:, kk * 128:(kk + 1) * 128],
                                             convin[:, j * CW + cb:j * CW + cb + 128],
                                             identb[:, :])
                        s_pe.inc(ti)
                @block.vector
                def _(v):
                    for r in range(8):
                        v.wait_ge(s_pe.h, pe_b_xm + r + 1)
                        s_ve.inc(v.tensor_copy(x_m[:, r * 512:(r + 1) * 512], pxb[r % 2]))
                assert s_ve.n == ve_b_xm + 8
                xm_ve = s_ve.n

                if c == 0:
                    tap("t_xmtok", x_m)
                    tap("t_acs", acsT[:, :])

                for g8 in range(8):
                    @block.vector
                    def _(v):
                        if c == 0 and g8 == 0:
                            v.wait_ge(s_gs2[0].h, gate_spilled[0])
                            v.wait_ge(s_gs2[1].h, gate_spilled[1])
                        if g8 >= 2:
                            v.wait_ge(s_pe.h, s16_cons[g8 - 2])
                        s16_m[g8] = s_ve.inc(v.tensor_copy(s16g[g8 % 2],
                                                           s_state[:, g8 * 512:(g8 + 1) * 512]))

                    @block.gpsimd
                    def _(g):
                        g.wait_ge(s_ve.h, xm_ve)
                        if c == 0 and g8 == 0:
                            g.wait_ge(s_gs2[0].h, gate_spilled[0])
                            g.wait_ge(s_gs2[1].h, gate_spilled[1])
                        if g8 >= 2:
                            g.wait_ge(s_pe.h, mt_cons[(g8 - 2) * 8 + 7])
                        dv = dh_rep[:, g8 * 8:(g8 + 1) * 8].unsqueeze(2).broadcast_to((128, 8, 64))
                        s_gp.inc(g.tensor_tensor(r3(dres[g8 % 2], 64),
                                                 r3(x_m[:, g8 * 512:(g8 + 1) * 512], 64), dv, ALU.mult))
                        g.wait_ge(s_ve.h, prep_ve)
                        wv = r3(stok[:, 128:192], 1)[:, g8 * 8:(g8 + 1) * 8, :].broadcast_to((128, 8, 64))
                        s_gp.inc(g.tensor_tensor(r3(xcw[g8 % 2], 64),
                                                 r3(x_m[:, g8 * 512:(g8 + 1) * 512], 64), wv, ALU.mult))
                        g.wait_ge(s_ac.h, prep_ac)
                        if g8 >= 1:
                            g.wait_ge(s_ve.h, supd[g8 - 1])
                        dcv = decr[:, g8 * 8:(g8 + 1) * 8].unsqueeze(2).broadcast_to((128, 8, 64))
                        s_gp.inc(g.tensor_tensor(r3(sdec[g8 % 2], 64),
                                                 r3(s_state[:, g8 * 512:(g8 + 1) * 512], 64), dcv, ALU.mult))
                    pool_gp = s_gp.n

                    @block.tensor
                    def _(t):
                        if g8 == 0:
                            t.wait_ge(s_sp.h, flat_sp)
                        if g8 >= 1:
                            t.wait_ge(s_ve.h, dneg_m[g8 - 1])
                        t.matmul(pA[:, 0:512], ones2[:, :], flat[:, g8 * 1024:g8 * 1024 + 512],
                                 start=True, stop=True)
                        s_pe.inc(t.matmul(pA[:, 512:1024], ones2[:, :],
                                          flat[:, g8 * 1024 + 512:(g8 + 1) * 1024],
                                          start=True, stop=True))
                    arep_pe = s_pe.n
                    if g8 == 7:
                        prev_flat_pe = arep_pe

                    @block.vector
                    def _(v):
                        v.wait_ge(s_pe.h, arep_pe)
                        if g8 >= 2:
                            v.wait_ge(s_ac.h, e_m[g8 - 2])
                        av = r3(stok[:, 64:128], 1)[:, g8 * 8:(g8 + 1) * 8, :].broadcast_to((128, 8, 128))
                        v.tensor_tensor(r3(dnegb[g8 % 2], 128), r3(pA[:, :], 128), av, ALU.subtract)
                        dneg_m[g8] = s_ve.inc(v.tensor_scalar(dnegb[g8 % 2], dnegb[g8 % 2],
                                                              0.0, None, ALU.min))

                    @block.scalar
                    def _(sc):
                        sc.wait_ge(s_ve.h, dneg_m[g8])
                        if g8 >= 2:
                            sc.wait_ge(s_ve.h, mt_cons_ve[(g8 - 2) * 8 + 7])
                        e_m[g8] = s_ac.inc(sc.activation(e_sb[g8 % 2], dnegb[g8 % 2], AF.Exp))

                    @block.tensor
                    def _(t):
                        t.wait_ge(s_ve.h, s16_m[g8])
                        if g8 >= 1:
                            t.wait_ge(s_ve.h, yevac[g8 - 1])
                        t.matmul(p2[:, :], convin[:, 33 * CW + cb:33 * CW + cb + 128],
                                 s16g[g8 % 2], start=True, stop=True)
                        t.wait_ge(s_gp.h, pool_gp)
                        s_pe.inc(t.matmul(p1[:, :], identb[:, :], dres[g8 % 2],
                                          start=True, stop=False, skip_group_check=True))
                    yoff_pe = s_pe.n
                    s16_cons[g8] = yoff_pe

                    for hh in range(8):
                        u = g8 * 8 + hh
                        @block.vector
                        def _(v):
                            v.wait_ge(s_ac.h, e_m[g8])
                            if u >= 2:
                                v.wait_ge(s_pe.h, mt_cons[u - 2])
                            mt_cons_ve[u] = s_ve.inc(
                                v.scalar_tensor_tensor(mt[u % 2], gm[:, :],
                                                       stok[:, u:u + 1],
                                                       e_sb[g8 % 2][:, hh * 128:(hh + 1) * 128],
                                                       ALU.mult, ALU.mult))
                        @block.tensor
                        def _(t):
                            t.wait_ge(s_ve.h, mt_cons_ve[u])
                            mt_cons[u] = s_pe.inc(
                                t.matmul(p1[:, hh * 64:(hh + 1) * 64], mt[u % 2],
                                         x_m[:, u * 64:(u + 1) * 64],
                                         start=False, stop=(hh == 7), skip_group_check=True))
                    ydiag_pe = s_pe.n

                    @block.tensor
                    def _(t):
                        if g8 >= 1:
                            t.wait_ge(s_ve.h, supd[g8 - 1])
                        s_pe.inc(t.matmul(p3[:, :], b_tok[:, :], xcw[g8 % 2],
                                          start=True, stop=True))
                    st_pe = s_pe.n

                    @block.vector
                    def _(v):
                        v.wait_ge(s_pe.h, yoff_pe)
                        if g8 == 0:
                            v.wait_ge(s_ac.h, prep_ac)
                        ev = expa[:, g8 * 8:(g8 + 1) * 8].unsqueeze(2).broadcast_to((128, 8, 64))
                        v.tensor_tensor(r3(tmpg, 64), r3(p2[:, :], 64), ev, ALU.mult)
                        v.wait_ge(s_pe.h, ydiag_pe)
                        yevac[g8] = s_ve.inc(
                            v.tensor_tensor(y_tok[:, g8 * 512:(g8 + 1) * 512],
                                            tmpg, p1[:, :], ALU.add))
                        v.wait_ge(s_pe.h, st_pe)
                        supd[g8] = s_ve.inc(
                            v.tensor_tensor(s_state[:, g8 * 512:(g8 + 1) * 512],
                                            sdec[g8 % 2], p3[:, :], ALU.add))
                y_ve = s_ve.n
                if c == 0:
                    tap("t_y", y_tok)
                if c == 3:
                    tap("t_S", s_state[:, :])

                if c < 3:
                    @block.sync
                    def _(sync):
                        sync.wait_ge(s_ve.h, y_ve)
                        sync.wait_ge(s_pe.h, last_yn_pe.get((c + 1) % 2, 0))
                        gate_pref[c + 1] = s_gpre.inc(
                            sync.dma_start(out=gate_ch[(c + 1) % 2],
                                           in_=gate_sp[(c + 1) * 128:(c + 2) * 128, :]), 16)

                @block.vector
                def _(v):
                    v.wait_ge(s_gpre.h, gate_pref[c])
                    v.wait_ge(s_ve.h, y_ve)
                    s_ve.inc(v.tensor_tensor(y_tok, y_tok, gate_ch[c % 2], ALU.mult))
                    for s in range(8):
                        v.bn_stats(bn6[:, s * 6:(s + 1) * 6], y_tok[:, s * 512:(s + 1) * 512])
                    v.tensor_copy(spacr[:, 0:32], identf[:, 0:32])
                    v.tensor_copy(spacr[:, 0:32], identf[:, 32:64])
                    v.tensor_copy(spacr[:, 0:32], identf[:, 64:96])
                    v.bn_aggr(tiny[:, 0:2], bn6[:, 0:48])
                    v.tensor_copy(spacr[:, 0:32], identf[:, 0:32])
                    v.tensor_copy(spacr[:, 0:32], identf[:, 32:64])
                    v.tensor_copy(spacr[:, 0:32], identf[:, 64:96])
                    v.scalar_tensor_tensor(tiny[:, 3:4], tiny[:, 0:1], tiny[:, 0:1],
                                           tiny[:, 1:2], ALU.mult, ALU.add)
                    v.tensor_copy(spacr[:, 0:32], identf[:, 0:32])
                    v.tensor_copy(spacr[:, 0:32], identf[:, 32:64])
                    s_ve.inc(v.tensor_copy(spacr[:, 0:32], identf[:, 64:96]))
                rms_ve = s_ve.n
                @block.scalar
                def _(sc):
                    sc.wait_ge(s_ve.h, rms_ve)
                    sc.activation(tiny[:, 5:6], tiny[:, 3:4], AF.Ln, bias=epsc[:, :])
                    s_ac.inc(sc.activation(tiny[:, 5:6], tiny[:, 5:6], AF.Exp, scale=-0.5))
                rms_ac = s_ac.n
                if c == 3 and "t_dbg_rms" in taps:
                    @block.gpsimd
                    def _(g):
                        g.wait_ge(s_ac.h, rms_ac)
                        s_sp.inc(g.dma_start(out=taps["t_dbg_rms"][:, 0:8], in_=tiny[:, :]), 16)
                        s_sp.inc(g.dma_start(out=taps["t_dbg_rms"][:, 8:56], in_=bn6[:, 0:48]), 16)
                    @block.vector
                    def _(v):
                        v.wait_ge(s_sp.h, s_sp.n)
                @block.vector
                def _(v):
                    v.wait_ge(s_ac.h, rms_ac)
                    if c > 0:
                        v.wait_ge(s_ynsp.h, ynsp[c - 1])
                    s_ve.inc(v.scalar_tensor_tensor(gate_ch[c % 2], y_tok, tiny[:, 5:6],
                                                    nw_rep[:, :], ALU.mult, ALU.mult))
                yn_ve = s_ve.n
                if c == 3:
                    tap("t_yn", gate_ch[c % 2])

                pe_b_yn = s_pe.n
                ve_b_yn = s_ve.n
                @block.tensor
                def _(t):
                    t.wait_ge(s_ve.h, yn_ve)
                    for r in range(8):
                        if r >= 2:
                            t.wait_ge(s_ve.h, ve_b_yn + r - 1)
                        dstp = pxb[r % 2]
                        ti = None
                        for kk in range(4):
                            k = r * 4 + kk
                            ti = t.transpose(dstp[:, kk * 128:(kk + 1) * 128],
                                             gate_ch[c % 2][:, k * 128:(k + 1) * 128], identb[:, :])
                        s_pe.inc(ti)
                @block.vector
                def _(v):
                    for r in range(8):
                        v.wait_ge(s_pe.h, pe_b_yn + r + 1)
                        s_ve.inc(v.tensor_copy(ynT_ch[:, r * 512:(r + 1) * 512], pxb[r % 2]))
                ynT_ve = s_ve.n
                prev_ynT_ve = ynT_ve
                last_yn_pe[c % 2] = s_pe.n
                @block.sync
                def _(sync):
                    sync.wait_ge(s_ve.h, ynT_ve)
                    dst = ynT_sp.rearrange("(k p) t -> p k t", p=128)[:, :, c * 128:(c + 1) * 128]
                    ynsp[c] = s_ynsp.inc(sync.dma_start(out=dst, in_=r3(ynT_ch, 128)), 16)
            ynT_spld = s_ynsp.n
            tap("t_ynT", ynT_sp)

            if p == 0 and late_cast:
                emit_cast(late_cast)
                late_cast = []

            if STAGE < 4:
                _store(block, (s_ve, s_ac), s_out, x_res, out, bi, r0)
                continue

            pbanks = [pA[:, 0:512], pA[:, 512:1024], p1[:, :], p2[:, :], p3[:, :],
                      p4[:, :], p5[:, :], pb16[:, :].bitcast(F32)]
            opv = {}
            opv_ac = {}
            for half in range(2):
                pe_b_o = s_pe.n
                ymark = {}
                wmark_o = {}
                @block.sync
                def _(sync):
                    sync.wait_ge(s_ynsp.h, ynT_spld)
                    sync.wait_ge(s_pe.h, s_pe.n)
                    for kt in range(32):
                        sl = kt % 4
                        if kt >= 4:
                            sync.wait_ge(s_pe.h, pe_b_o + kt - 3)
                        ymark[kt] = s_y4[sl].inc(sync.dma_start(out=ynstr[sl],
                                                 in_=ynT_sp[kt * 128:(kt + 1) * 128, :]), 16)
                @block.scalar
                def _(sc):
                    wait_cast(sc, "woutT16")
                    sc.wait_ge(s_pe.h, pe_b_o)
                    for kt in range(32):
                        sl = kt % 3
                        if kt >= 3:
                            sc.wait_ge(s_pe.h, pe_b_o + kt - 2)
                        src = wtile3(woutT16, kt * 128, 1, half * 1024, 1024)
                        wmark_o[kt] = s_w3[sl].inc(
                            sc.dma_start(out=r3(warena[sl][:, 0:1024], 1024), in_=src), 16)

                @block.tensor
                def _(t):
                    t.wait_ge(s_ve.h, s_ve.n)
                    t.wait_ge(s_ac.h, s_ac.n)
                    for kt in range(32):
                        t.wait_ge(s_y4[kt % 4].h, ymark[kt])
                        t.wait_ge(s_w3[kt % 3].h, wmark_o[kt])
                        ti = None
                        for tt in range(NT):
                            for db in range(2):
                                ti = t.matmul(pbanks[tt * 2 + db],
                                              ynstr[kt % 4][:, tt * 128:(tt + 1) * 128],
                                              warena[kt % 3][:, db * 512:(db + 1) * 512],
                                              start=(kt == 0), stop=(kt == 31),
                                              skip_group_check=True)
                        s_pe.inc(ti)
                op_mm = s_pe.n

                for b in range(8):
                    tt, db = b // 2, b % 2
                    dst = attn_tok[:, tt * D + half * 1024 + db * 512:
                                   tt * D + half * 1024 + (db + 1) * 512]
                    if b % 2 == 0:
                        @block.vector
                        def _(v):
                            v.wait_ge(s_pe.h, op_mm)
                            s_ve.inc(v.tensor_copy(dst, pbanks[b]))
                    else:
                        @block.scalar
                        def _(sc):
                            sc.wait_ge(s_pe.h, op_mm)
                            s_ac.inc(sc.activation(dst, pbanks[b], AF.Copy))
                opv[half] = s_ve.n
                opv_ac[half] = s_ac.n
            tap("t_attn", attn_tok)

            @block.vector
            def _(v):
                v.wait_ge(s_ac.h, opv_ac[1])
            ln_stats(attn_tok)
            ln2_ac = s_ac.n
            @block.vector
            def _(v):
                v.wait_ge(s_ac.h, ln2_ac)
                for i in range(NT):
                    v.tensor_scalar(attn_tok[:, i * D:(i + 1) * D], attn_tok[:, i * D:(i + 1) * D],
                                    mv[:, 2 * i:2 * i + 1], rstd[:, i:i + 1],
                                    ALU.subtract, ALU.mult)
                    s_ve.inc(v.tensor_tensor(x_res[:, i * D:(i + 1) * D],
                                             x_res[:, i * D:(i + 1) * D],
                                             attn_tok[:, i * D:(i + 1) * D], ALU.add))
            tap("t_x1", x_res[:, :])

            if STAGE < 5:
                _store(block, (s_ve,), s_out, x_res, out, bi, r0)
                continue

            @block.vector
            def _(v):
                v.wait_ge(s_ld.h, ynT_spld)
            ln_stats(x_res)
            @block.tensor
            def _(t):
                t.wait_ge(s_ve.h, s_ve.n)
            ln_apply_transpose(x_res)
            xn2_ve = s_ve.n

            pe_b_f = s_pe.n
            wmark_f = {}
            @block.sync
            def _(sc):
                wait_cast(sc, "wgT16")
                wait_cast(sc, "wuT16")
                sc.wait_ge(s_pe.h, pe_b_f)
                for j in range(NJT):
                    sl0 = (2 * j) % 3
                    sl1 = (2 * j + 1) % 3
                    if j >= 2:
                        sc.wait_ge(s_pe.h, pe_b_f + j - 1)
                    wmark_f[2 * j] = s_w3[sl0].inc(sc.dma_start(out=r3(warena[sl0], 128),
                                                   in_=wtile3(wgT16, 0, KD, j * 128, 128)), 16)
                    if j >= 1:
                        sc.wait_ge(s_pe.h, pe_b_f + j)
                    wmark_f[2 * j + 1] = s_w3[sl1].inc(sc.dma_start(out=r3(warena[sl1], 128),
                                                       in_=wtile3(wuT16, 0, KD, j * 128, 128)), 16)

            for j in range(NJT):
                pg = p1 if j % 2 == 0 else p2
                pu = p3 if j % 2 == 0 else p4
                @block.tensor
                def _(t):
                    t.wait_ge(s_w3[(2 * j) % 3].h, wmark_f[2 * j])
                    t.wait_ge(s_w3[(2 * j + 1) % 3].h, wmark_f[2 * j + 1])
                    if j < 2:
                        t.wait_ge(s_ve.h, xn2_ve)
                    else:
                        t.wait_ge(s_ve.h, h_m[j - 2])
                        t.wait_ge(s_ac.h, sg_m[j - 2])
                    for k in range(KD):
                        t.matmul(pg[:, :], warena[(2 * j) % 3][:, k * 128:(k + 1) * 128],
                                 xnT[:, k * TP:(k + 1) * TP], start=(k == 0), stop=(k == KD - 1))
                    ti = None
                    for k in range(KD):
                        ti = t.matmul(pu[:, :], warena[(2 * j + 1) % 3][:, k * 128:(k + 1) * 128],
                                      xnT[:, k * TP:(k + 1) * TP], start=(k == 0), stop=(k == KD - 1))
                    s_pe.inc(ti)
                fmm = s_pe.n
                if j == 0 and "t_dbg_g" in taps:
                    @block.vector
                    def _(v):
                        v.wait_ge(s_pe.h, fmm)
                        s_ve.inc(v.tensor_copy(attn_tok[:, 0:512], pg[:, :]))
                        s_ve.inc(v.tensor_copy(attn_tok[:, 512:1024], pu[:, :]))
                    dbgv = s_ve.n
                    @block.gpsimd
                    def _(sync):
                        sync.wait_ge(s_ve.h, dbgv)
                        s_sp.inc(sync.dma_start(out=taps["t_dbg_g"], in_=attn_tok[:, 0:512]), 16)
                        s_sp.inc(sync.dma_start(out=taps["t_dbg_u"], in_=attn_tok[:, 512:1024]), 16)
                    @block.scalar
                    def _(sc):
                        sc.wait_ge(s_sp.h, s_sp.n)
                @block.scalar
                def _(sc):
                    sc.wait_ge(s_pe.h, fmm)
                    sg_m[j] = s_ac.inc(sc.activation(sg_j[j % 2], pg[:, :], AF.Silu,
                                                     scale=gs_c[:, j:j + 1]))
                if j == 0 and "t_dbg_sg" in taps:
                    @block.gpsimd
                    def _(sync):
                        sync.wait_ge(s_ac.h, sg_m[0])
                        s_sp.inc(sync.dma_start(out=taps["t_dbg_sg"], in_=sg_j[0]), 16)
                    @block.vector
                    def _(v):
                        v.wait_ge(s_sp.h, s_sp.n)
                @block.vector
                def _(v):
                    v.wait_ge(s_ac.h, sg_m[j])
                    h_m[j] = s_ve.inc(
                        v.scalar_tensor_tensor(hT(j), pu[:, :],
                                               us_c[:, j:j + 1], sg_j[j % 2],
                                               ALU.mult, ALU.mult))
            h_ve = s_ve.n
            if "t_h" in taps:
                tap("t_h", convin_b[:, 0:32 * TP])

            pe_b_d = s_pe.n
            sg_last = sg_m[NJT - 1]
            wmark_d = {}
            @block.sync
            def _(sync):
                sync.wait_ge(s_pe.h, pe_b_d)
                s_dsr.inc(sync.dma_start(out=ds_rep, in_=dsr_in), 16)
            dsr_m = s_dsr.n
            @block.scalar
            def _(sc):
                wait_cast(sc, "wdT16")
                sc.wait_ge(s_pe.h, pe_b_d)
                for idx in range(2 * NJT):
                    half, k = idx // NJT, idx % NJT
                    sl = idx % 3
                    if idx >= 3:
                        sc.wait_ge(s_pe.h, pe_b_d + idx - 2)
                    src = wtile3(wdT16, k * 128, 1, half * 1024, 1024)
                    wmark_d[idx] = s_w3[sl].inc(
                        sc.dma_start(out=r3(warena[sl][:, 0:1024], 1024), in_=src), 16)

            dnv = {}
            for half in range(2):
                @block.tensor
                def _(t):
                    if half == 0:
                        t.wait_ge(s_ve.h, h_ve)
                        t.wait_ge(s_ac.h, sg_last)
                    else:
                        t.wait_ge(s_ve.h, dnv[0])
                    for k in range(NJT):
                        idx = half * NJT + k
                        t.wait_ge(s_w3[idx % 3].h, wmark_d[idx])
                        ti = None
                        for tt in range(NT):
                            for db in range(2):
                                ti = t.matmul(pbanks[tt * 2 + db],
                                              hT(k)[:, tt * 128:(tt + 1) * 128],
                                              warena[idx % 3][:, db * 512:(db + 1) * 512],
                                              start=(k == 0), stop=(k == NJT - 1),
                                              skip_group_check=True)
                        s_pe.inc(ti)
                dmm = s_pe.n
                @block.vector
                def _(v):
                    v.wait_ge(s_pe.h, dmm)
                    if half == 0:
                        v.wait_ge(s_dsr.h, dsr_m)
                    for b in range(8):
                        tt, db = b // 2, b % 2
                        dst = attn_tok[:, tt * D + half * 1024 + db * 512:
                                       tt * D + half * 1024 + (db + 1) * 512]
                        s_ve.inc(v.tensor_tensor(dst, pbanks[b],
                                                 ds_rep[:, half * 1024 + db * 512:
                                                        half * 1024 + (db + 1) * 512],
                                                 ALU.mult))
                dnv[half] = s_ve.n

            ln_stats(attn_tok)
            ln4_ac = s_ac.n
            @block.vector
            def _(v):
                v.wait_ge(s_ac.h, ln4_ac)
                for i in range(NT):
                    v.tensor_scalar(attn_tok[:, i * D:(i + 1) * D], attn_tok[:, i * D:(i + 1) * D],
                                    mv[:, 2 * i:2 * i + 1], rstd[:, i:i + 1],
                                    ALU.subtract, ALU.mult)
                    s_ve.inc(v.tensor_tensor(x_res[:, i * D:(i + 1) * D],
                                             x_res[:, i * D:(i + 1) * D],
                                             attn_tok[:, i * D:(i + 1) * D], ALU.add))
            _store(block, (s_ve,), s_out, x_res, out, bi, r0)

        @block.sync
        def _(sync):
            sync.wait_ge(s_out.h, s_out.n)
            sync.wait_ge(s_ld.h, s_ld.n)
            sync.wait_ge(s_sp.h, s_sp.n)
            sync.wait_ge(s_gpre.h, s_gpre.n)
            sync.wait_ge(s_ynsp.h, s_ynsp.n)
            sync.wait_ge(s_dsr.h, s_dsr.n)
            for sm in s_w3 + s_y4 + s_gs2 + s_cl2 + s_cw2:
                sync.wait_ge(sm.h, sm.n)

    return nc, taps


def _store(block, sems, s_out, x_res, out, bi, r0):
    @block.sync
    def _(sync):
        for sm in sems:
            sync.wait_ge(sm.h, sm.n)
        for i in range(NT):
            s_out.inc(sync.dma_start(out=out[bi, r0 + i * 128:r0 + (i + 1) * 128, :],
                                     in_=x_res[:, i * D:(i + 1) * D]), 16)


def _layout_inputs(inputs):
    f32 = np.float32
    asC = np.ascontiguousarray
    w = {}
    w["winT"] = asC(np.asarray(inputs["L0_attn_in_proj_w"], f32).T)
    w["woutT"] = asC(np.asarray(inputs["L0_attn_out_proj_w"], f32).T)
    w["wgT"] = asC(np.asarray(inputs["L0_ffn_gate_w"], f32).T)
    w["wuT"] = asC(np.asarray(inputs["L0_ffn_up_w"], f32).T)
    w["wdT"] = asC(np.asarray(inputs["L0_ffn_down_w"], f32).T)
    w["convw"] = asC(np.asarray(inputs["L0_attn_conv_w"], f32).reshape(NCT, 128, 4)
                     .transpose(1, 0, 2).reshape(128, NCT * 4))
    w["convb"] = asC(np.asarray(inputs["L0_attn_conv_b"], f32).reshape(NCT, 128).T)
    w["dtb"] = asC(np.asarray(inputs["L0_attn_dt_bias"], f32).reshape(NH, 1))
    w["alog"] = asC(np.asarray(inputs["L0_attn_A_log"], f32).reshape(NH, 1))
    w["dh_rep"] = asC(np.broadcast_to(np.asarray(inputs["L0_attn_D"], f32)[None, :], (128, NH)))
    w["nw_rep"] = asC(np.broadcast_to(np.asarray(inputs["L0_attn_norm_w"], f32)[None, :], (128, DIN)))
    w["gs"] = asC(np.asarray(inputs["L0_ffn_gate_s"], f32).reshape(NJT, 128).T)
    w["us"] = asC(np.asarray(inputs["L0_ffn_up_s"], f32).reshape(NJT, 128).T)
    w["ds_rep"] = asC(np.broadcast_to(np.asarray(inputs["L0_ffn_down_s"], f32)[None, :],
                                      (128, D)))
    return w


def kernel(**inputs):
    x = np.asarray(inputs["L0_attn_hidden_states"], np.float32)
    w = _layout_inputs(inputs)
    nc, _ = build_program()
    in_maps = []
    for c in range(8):
        m = dict(w)
        m["x"] = np.ascontiguousarray(x[2 * c:2 * c + 2])
        in_maps.append(m)
    res = run_bass_kernel_spmd(nc, in_maps, core_ids=list(range(8)))
    return np.concatenate([r["out"] for r in res.results], axis=0)


# version 08
# speedup vs baseline: 1.0276x; 1.0276x over previous
import numpy as np
import ml_dtypes
from contextlib import ExitStack
import concourse.bass as bass
import concourse.mybir as mybir
from concourse.bass_utils import run_bass_kernel_spmd

F32 = mybir.dt.float32
BF16 = mybir.dt.bfloat16
AF = mybir.ActivationFunctionType
ALU = mybir.AluOpType

D = 2048
KD = D // 128
DIN = 4096
CONVD = 4352
NCT = CONVD // 128
CW = 515
PROJ = 8512
NH = 64
INTER = 5632
NJT = INTER // 128
TP = 512
NT = TP // 128
EPS = 1e-5
CCH = 2128

N_PASS = 8
STAGE = 6
TAPS = ()


def build_program():
    nc = bass.Bass("TRN2", target_bir_lowering=False, detect_race_conditions=False)

    x_in = nc.dram_tensor("x", [2, 2048, D], F32, kind="ExternalInput").ap()
    wing = nc.dram_tensor("win_g", [32, 128, 4 * 512], F32, kind="ExternalInput").ap()
    winx = nc.dram_tensor("win_x", [35, 128, KD * 128], F32, kind="ExternalInput").ap()
    woutT = nc.dram_tensor("woutT", [DIN, D], F32, kind="ExternalInput").ap()
    wgs = nc.dram_tensor("wg_s", [NJT, 128, KD * 128], F32, kind="ExternalInput").ap()
    wus = nc.dram_tensor("wu_s", [NJT, 128, KD * 128], F32, kind="ExternalInput").ap()
    wdT = nc.dram_tensor("wdT", [INTER, D], F32, kind="ExternalInput").ap()
    convw_in = nc.dram_tensor("convw", [128, NCT * 4], F32, kind="ExternalInput").ap()
    convb_in = nc.dram_tensor("convb", [128, NCT], F32, kind="ExternalInput").ap()
    dtb_in = nc.dram_tensor("dtb", [NH, 1], F32, kind="ExternalInput").ap()
    alog_in = nc.dram_tensor("alog", [NH, 1], F32, kind="ExternalInput").ap()
    dh_rep_in = nc.dram_tensor("dh_rep", [128, NH], F32, kind="ExternalInput").ap()
    nw_rep_in = nc.dram_tensor("nw_rep", [128, DIN], F32, kind="ExternalInput").ap()
    gs_in = nc.dram_tensor("gs", [128, NJT], F32, kind="ExternalInput").ap()
    us_in = nc.dram_tensor("us", [128, NJT], F32, kind="ExternalInput").ap()
    dsr_in = nc.dram_tensor("ds_rep", [128, D], F32, kind="ExternalInput").ap()
    out = nc.dram_tensor("out", [2, 2048, D], F32, kind="ExternalOutput").ap()

    wing16 = nc.dram_tensor("win_g16", [32, 128, 4 * 512], BF16, kind="Internal").ap()
    winx16 = nc.dram_tensor("win_x16", [35, 128, KD * 128], BF16, kind="Internal").ap()
    woutT16 = nc.dram_tensor("woutT16", [DIN, D], BF16, kind="Internal").ap()
    wgs16 = nc.dram_tensor("wg_s16", [NJT, 128, KD * 128], BF16, kind="Internal").ap()
    wus16 = nc.dram_tensor("wu_s16", [NJT, 128, KD * 128], BF16, kind="Internal").ap()
    wdT16 = nc.dram_tensor("wdT16", [INTER, D], BF16, kind="Internal").ap()
    gate_sp = nc.dram_tensor("gate_sp", [TP, DIN], BF16, kind="Internal").ap()
    ynT_sp3 = nc.dram_tensor("ynT_sp", [4, 128, 32, 128], BF16, kind="Internal").ap()

    tap_specs = {
        "t_xn1T": ([128, KD * TP], BF16),
        "t_gate": ([TP, DIN], BF16),
        "t_convin": ([128, NCT * CW], BF16),
        "t_dtraw": ([NH, TP], F32),
        "t_dtsp": ([NH, TP], F32),
        "t_convout": ([128, NCT * CW], BF16),
        "t_acs": ([NH, 128], F32),
        "t_xmtok": ([128, DIN], BF16),
        "t_y": ([128, DIN], BF16),
        "t_yn": ([128, DIN], BF16),
        "t_ynT": ([DIN, TP], BF16),
        "t_attn": ([128, NT * D], F32),
        "t_x1": ([128, NT * D], F32),
        "t_h": ([128, 32 * TP], BF16),
        "t_dbg_sg": ([128, TP], BF16),
        "t_dbg_rms": ([128, 64], F32),
        "t_dbg_g": ([128, TP], F32),
        "t_dbg_u": ([128, TP], F32),
        "t_S": ([128, DIN], F32),
    }
    taps = {}
    for name in TAPS:
        shape, dt = tap_specs[name]
        taps[name] = nc.dram_tensor(name, shape, dt, kind="ExternalOutput").ap()

    es = ExitStack()
    sbuf = lambda n, s, d: es.enter_context(nc.sbuf_tensor(n, s, d))
    psal = lambda n, s, d: es.enter_context(nc.psum_tensor(n, s, d))

    with es:
        x_res = sbuf("x_res", [128, NT * D], F32)
        xn_tok_b = sbuf("xn_tok", [128, 2 * D], BF16)
        xn_tok = [xn_tok_b[:, i * D:(i + 1) * D] for i in range(2)]
        ynT_ch = xn_tok_b[:, 0:DIN]
        xnT_b = sbuf("xnT", [128, KD * TP], BF16)
        xnT = xnT_b[:, :]
        flat = xnT_b[0:2, :]
        convin_b = sbuf("convin", [128, NCT * CW], BF16)
        convin = convin_b[:, :]
        tail = sbuf("tail", [128, NCT * 3], BF16)
        dt_raw = sbuf("dt_raw", [NH, TP], F32)
        dt_sp = sbuf("dt_sp", [NH, TP], F32)
        sp_scr2 = sbuf("sp_scr2", [NH, TP], F32)
        a_dt = sp_scr2[:, :]
        acsT = sbuf("acsT", [NH, 128], F32)
        hilo = sbuf("hilo", [NH, 256], BF16)
        wdecT = sbuf("wdecT", [NH, 128], F32)
        dtwT = sbuf("dtwT", [NH, 128], F32)
        stok = sbuf("stok", [128, 192], F32)
        expa = sbuf("expa", [128, NH], F32)
        decr = sbuf("decr", [128, NH], F32)
        b_tok = sbuf("b_tok", [128, 128], BF16)
        gm = sbuf("gm", [128, 128], BF16)
        s_state = sbuf("s_state", [128, DIN], F32)

        A1 = sbuf("A1", [128, 16384], BF16)
        x_m = A1[:, 0:4096]
        y_tok = A1[:, 4096:8192]
        dnegb = [A1[:, 8192 + i * 2048: 8192 + (i + 1) * 2048].bitcast(F32) for i in range(2)]
        e_sb = [A1[:, 12288 + i * 1024: 12288 + (i + 1) * 1024] for i in range(2)]
        sdec = [A1[:, 14336 + i * 1024: 14336 + (i + 1) * 1024].bitcast(F32) for i in range(2)]
        attn_tok = A1[:, :].bitcast(F32)

        A2 = sbuf("A2", [128, 4608], BF16)
        mt = [A2[:, i * 128:(i + 1) * 128] for i in range(2)]
        dres = [A2[:, 256 + i * 512: 256 + (i + 1) * 512] for i in range(2)]
        xcw = [A2[:, 1280 + i * 512: 1280 + (i + 1) * 512] for i in range(2)]
        tmpg = A2[:, 2304:3328].bitcast(F32)
        s16g = [A2[:, 3328 + i * 512: 3328 + (i + 1) * 512] for i in range(2)]
        zerb = A2[:, 4352:4608].bitcast(F32)
        zer = zerb[0:NH, :]
        gstag = [A2[:, i * 512:(i + 1) * 512] for i in range(2)]
        diagw = [A2[:, 2048 + i * 128: 2048 + (i + 1) * 128] for i in range(8)]
        it1 = A2[:, 0:256].bitcast(F32)
        it2 = A2[:, 256:512].bitcast(F32)
        ynstr = [A2[:, i * 512:(i + 1) * 512] for i in range(4)]
        aostr = [A2[:, 2048 + i * 1024: 2048 + (i + 1) * 1024].bitcast(F32) for i in range(2)]
        sg_j = [A2[:, i * 512:(i + 1) * 512] for i in range(2)]

        gate_ch_b = sbuf("gate_ch", [128, 2 * DIN], BF16)
        gate_ch = [gate_ch_b[:, i * DIN:(i + 1) * DIN] for i in range(2)]

        def hT(j):
            if j < 32:
                return convin_b[:, j * TP:(j + 1) * TP]
            return gate_ch_b[:, (j - 32) * TP:(j - 31) * TP]

        bn6 = sbuf("bn6", [128, 96], F32)
        mv = sbuf("mv", [128, 2 * NT], F32)
        spacr = sbuf("spacr", [128, 32], F32)
        rstd = sbuf("rstd", [128, NT], F32)
        vtmp = sbuf("vtmp", [128, NT], F32)
        epsc = sbuf("epsc", [128, 1], F32)
        tiny = sbuf("tiny", [128, 8], F32)
        warena_b = sbuf("warena", [128, 3 * 2048], BF16)
        warena = [warena_b[:, i * 2048:(i + 1) * 2048] for i in range(3)]
        wcast_b = sbuf("wcast", [128, 2 * CCH], BF16)
        wcast = [wcast_b[:, i * CCH:(i + 1) * CCH] for i in range(2)]
        identb = sbuf("identb", [128, 128], BF16)
        identf = sbuf("identf", [128, 128], F32)
        triu01 = sbuf("triu01", [128, 128], BF16)
        ones2 = sbuf("ones2", [2, 128], BF16)
        convw = sbuf("convw_s", [128, NCT * 4], F32)
        convb = sbuf("convb_s", [128, NCT], F32)
        dtb = sbuf("dtb_s", [NH, 1], F32)
        acol = sbuf("acol", [NH, 1], F32)
        dh_rep = sbuf("dh_rep_s", [128, NH], F32)
        nw_rep = sbuf("nw_rep_s", [128, DIN], BF16)
        gs_c = sbuf("gs_c", [128, NJT], F32)
        us_c = sbuf("us_c", [128, NJT], F32)
        ds_rep = xnT_b[:, :].bitcast(F32)[:, 0:D]

        pA = psal("pA", [128, 1024], F32)
        p1 = psal("p1", [128, 512], F32)
        p2 = psal("p2", [128, 512], F32)
        p3 = psal("p3", [128, 512], F32)
        p4 = psal("p4", [128, 512], F32)
        p5 = psal("p5", [128, 512], F32)
        pb16 = psal("pb16", [128, 1024], BF16)
        pt4 = [p1, p2, p3, p4]
        pxb = [pb16[:, 0:512], p5[:, :].bitcast(BF16)[:, 0:512]]

        class Sem:
            def __init__(self, name):
                self.h = es.enter_context(nc.semaphore(name))
                self.n = 0

            def inc(self, instr, k=1):
                instr.then_inc(self.h, k)
                self.n += k
                return self.n

        s_ld = Sem("s_ld")
        s_w3 = [Sem(f"s_w{i}") for i in range(3)]
        s_y4 = [Sem(f"s_y{i}") for i in range(4)]
        s_gs2 = [Sem(f"s_gs{i}") for i in range(2)]
        s_gpre = Sem("s_gpre")
        s_ynsp = Sem("s_ynsp")
        s_dsr = Sem("s_dsr")
        s_sp = Sem("s_sp")
        s_cl2 = [Sem(f"s_cl{i}") for i in range(2)]
        s_cw2 = [Sem(f"s_cw{i}") for i in range(2)]
        s_pe = Sem("s_pe")
        s_ve = Sem("s_ve")
        s_ac = Sem("s_ac")
        s_gp = Sem("s_gp")
        s_out = Sem("s_out")

        block = es.enter_context(nc.Block())

        def r3(ap2d, inner):
            return ap2d.rearrange("p (a b) -> p a b", b=inner)

        def wtile3(wT, row0, nkt, col0, wid):
            v = wT.rearrange("(a p) j -> p a j", p=128)
            return v[:, row0 // 128: row0 // 128 + nkt, col0:col0 + wid]

        @block.sync
        def _(sync):
            for dst, src in [
                (convw[:, :], convw_in), (convb[:, :], convb_in),
                (dtb[:, :], dtb_in), (acol[:, :], alog_in),
                (dh_rep[:, :], dh_rep_in),
                (gs_c[:, :], gs_in), (us_c[:, :], us_in),
            ]:
                s_ld.inc(sync.dma_start(out=dst, in_=src), 16)
        consts_ld = s_ld.n

        @block.gpsimd
        def _(g):
            s_sp.inc(g.dma_start(out=nw_rep[:, :], in_=nw_rep_in), 16)
            g.iota(it1[:, :], pattern=[[1, 128]], base=0, channel_multiplier=0,
                   allow_small_or_imprecise_dtypes=True)
            s_gp.inc(g.iota(it2[:, :], pattern=[[0, 128]], base=0, channel_multiplier=1,
                            allow_small_or_imprecise_dtypes=True))

        @block.vector
        def _(v):
            v.wait_ge(s_gp.h, s_gp.n)
            v.tensor_tensor(identf[:, :], it1[:, :], it2[:, :], ALU.is_equal)
            v.tensor_copy(identb[:, :], identf[:, :])
            v.tensor_tensor(triu01[:, :], it1[:, :], it2[:, :], ALU.is_ge)
            v.memset(ones2[:, :], 1.0)
            v.memset(epsc[:, :], EPS)
            v.memset(s_state[:, :], 0.0)
            v.memset(convin_b[:, :], 0.0)
            s_ve.inc(v.memset(tail[:, :], 0.0))

        @block.scalar
        def _(sc):
            sc.wait_ge(s_ld.h, consts_ld)
            s_ac.inc(sc.activation(acol[:, :], acol[:, :], AF.Exp))

        @block.vector
        def _(v):
            v.wait_ge(s_ac.h, s_ac.n)
            s_ve.inc(v.tensor_scalar(acol[:, :], acol[:, :], -1.0, None, ALU.mult))

        wready = {}
        cast_state = {"i": 0}

        def emit_cast(jobs):
            i0 = cast_state["i"]
            @block.gpsimd
            def _(g):
                for k, (key, src, dst) in enumerate(jobs):
                    i = i0 + k
                    sl = i % 2
                    w = src.shape[-1]
                    if i >= 2:
                        g.wait_ge(s_cw2[sl].h, s_cw2[sl].n)
                    s_cl2[sl].inc(g.dma_start(out=wcast[sl][:, 0:w], in_=src), 16)
                    g.wait_ge(s_cl2[sl].h, s_cl2[sl].n)
                    s_cw2[sl].inc(g.dma_start(out=dst, in_=wcast[sl][:, 0:w]), 16)
                    wready[key] = (sl, s_cw2[sl].n)
            cast_state["i"] = i0 + len(jobs)

        def wait_cast(eng, key):
            sl, cnt = wready[key]
            eng.wait_ge(s_cw2[sl].h, cnt)

        early_cast = []
        for s in range(32):
            early_cast.append((("g", s), wing[s:s + 1], wing16[s:s + 1]))
        for jt in range(34):
            early_cast.append((("x", jt), winx[jt:jt + 1], winx16[jt:jt + 1]))
        early_cast.append((("x", 34), winx[34:35, :, 0:KD * 64], winx16[34:35, :, 0:KD * 64]))
        emit_cast(early_cast)
        wout_cast = [(("wo", kt), woutT[kt * 128:(kt + 1) * 128, :],
                      woutT16[kt * 128:(kt + 1) * 128, :]) for kt in range(32)]
        late_cast = []
        for j in range(NJT):
            late_cast.append((("fg", j), wgs[j:j + 1], wgs16[j:j + 1]))
            late_cast.append((("fu", j), wus[j:j + 1], wus16[j:j + 1]))
        for k in range(NJT):
            late_cast.append((("wd", k), wdT[k * 128:(k + 1) * 128, :],
                              wdT16[k * 128:(k + 1) * 128, :]))

        def tap(name, src_ap):
            if name not in taps:
                return
            @block.sync
            def _(sync):
                sync.wait_ge(s_ve.h, s_ve.n)
                sync.wait_ge(s_ac.h, s_ac.n)
                sync.wait_ge(s_pe.h, s_pe.n)
                sync.wait_ge(s_gp.h, s_gp.n)
                sync.wait_ge(s_ld.h, s_ld.n)
                sync.wait_ge(s_sp.h, s_sp.n)
                s_out.inc(sync.dma_start(out=taps[name], in_=src_ap), 16)
            @block.vector
            def _(v):
                v.wait_ge(s_out.h, s_out.n)

        def ln_stats(src):
            nsub = D // 512
            @block.vector
            def _(v):
                for s in range(nsub):
                    for i in range(NT):
                        v.bn_stats(bn6[:, (i * nsub + s) * 6:(i * nsub + s + 1) * 6],
                                   src[:, i * D + s * 512:i * D + (s + 1) * 512])
                for i in range(NT):
                    v.bn_aggr(mv[:, 2 * i:2 * i + 2], bn6[:, i * nsub * 6:(i + 1) * nsub * 6])
                v.tensor_copy(spacr[:, 0:32], identf[:, 0:32])
                v.tensor_copy(spacr[:, 0:32], identf[:, 32:64])
                s_ve.inc(v.tensor_copy(spacr[:, 0:32], identf[:, 64:96]))

            @block.scalar
            def _(sc):
                sc.wait_ge(s_ve.h, s_ve.n)
                mvv = r3(mv[:, :], 2)[:, :, 1:2]
                sc.activation(r3(rstd[:, 0:NT], 1), mvv, AF.Ln, bias=epsc[:, :])
                s_ac.inc(sc.activation(rstd[:, 0:NT], rstd[:, 0:NT], AF.Exp, scale=-0.5))

        def ln_apply_transpose(src):
            stats_ac = s_ac.n
            pe_b = s_pe.n
            ve_b = s_ve.n

            def evac_mark(gr):
                return ve_b + 5 * (gr // 4) + 2 + (gr % 4)

            for i in range(NT):
                @block.vector
                def _(v):
                    v.wait_ge(s_ac.h, stats_ac)
                    if i >= 2:
                        v.wait_ge(s_pe.h, pe_b + 4 * (i - 1))
                    s_ve.inc(v.tensor_scalar(xn_tok[i % 2], src[:, i * D:(i + 1) * D],
                                             mv[:, 2 * i:2 * i + 1], rstd[:, i:i + 1],
                                             ALU.subtract, ALU.mult))

                @block.tensor
                def _(t):
                    t.wait_ge(s_ve.h, ve_b + 5 * i + 1)
                    for r in range(4):
                        gr = 4 * i + r
                        if gr >= 2:
                            t.wait_ge(s_ve.h, evac_mark(gr - 2))
                        dstp = pxb[gr % 2]
                        ti = None
                        for kk in range(4):
                            k = r * 4 + kk
                            ti = t.transpose(dstp[:, kk * 128:(kk + 1) * 128],
                                             xn_tok[i % 2][:, k * 128:(k + 1) * 128], identb[:, :])
                        s_pe.inc(ti)

                @block.vector
                def _(v):
                    for r in range(4):
                        gr = 4 * i + r
                        v.wait_ge(s_pe.h, pe_b + gr + 1)
                        dst = r3(xnT, TP)[:, r * 4:r * 4 + 4, i * 128:(i + 1) * 128]
                        s_ve.inc(v.tensor_copy(dst, r3(pxb[gr % 2], 128)))
                        assert s_ve.n == evac_mark(gr)
            assert s_pe.n == pe_b + 4 * NT and s_ve.n == ve_b + 5 * NT

        prev_flat_pe = 0
        prev_prep_ac = 0
        prev_ynT_ve = 0
        last_yn_pe = {}
        for p in range(N_PASS):
            bi = p // 4
            r0 = (p % 4) * TP
            first = (p % 4) == 0

            spill_g = {}
            gstag_g = {}
            evac_g = {}
            evac_x = {}
            diag_m = {}
            convev = {}
            gate_pref = {}
            ynsp = {}
            dneg_m = {}
            e_m = {}
            mt_cons = {}
            mt_cons_ve = {}
            yevac = {}
            supd = {}
            tr_m = {}
            h_m = {}
            sg_m = {}
            dv_m = {}
            dtr_m = {}
            s16_m = {}
            s16_cons = {}

            @block.sync
            def _(sync):
                if p > 0:
                    sync.wait_ge(s_out.h, s_out.n)
                for i in range(NT):
                    s_ld.inc(sync.dma_start(out=x_res[:, i * D:(i + 1) * D],
                                            in_=x_in[bi, r0 + i * 128:r0 + (i + 1) * 128, :]), 16)
            xload = s_ld.n

            @block.vector
            def _(v):
                v.wait_ge(s_ld.h, xload)
            ln_stats(x_res)
            @block.tensor
            def _(t):
                t.wait_ge(s_ve.h, s_ve.n)
            ln_apply_transpose(x_res)
            xnT_ve = s_ve.n
            tap("t_xn1T", xnT)

            pe_b_g = s_pe.n
            wmark_g = {}

            def load_gate_quarter(sc, idx):
                sl = idx % 3
                if p == 0:
                    wait_cast(sc, ("g", idx))
                if idx >= 3:
                    sc.wait_ge(s_pe.h, pe_b_g + idx - 2)
                wmark_g[idx] = (sl, s_w3[sl].inc(
                    sc.dma_start(out=warena[sl][:, :], in_=wing16[idx:idx + 1]), 16))

            @block.scalar
            def _(sc):
                for idx in range(3):
                    load_gate_quarter(sc, idx)

            for jc in range(8):
                @block.scalar
                def _(sc):
                    for hh in range(4):
                        idx = jc * 4 + hh + 3
                        if idx < 32:
                            load_gate_quarter(sc, idx)

                @block.tensor
                def _(t):
                    if jc == 0:
                        t.wait_ge(s_ve.h, xnT_ve)
                    if jc >= 1:
                        t.wait_ge(s_ac.h, evac_g[jc - 1])
                    for hh in range(4):
                        idx = jc * 4 + hh
                        sl, wv = wmark_g[idx]
                        t.wait_ge(s_w3[sl].h, wv)
                        ti = None
                        for tt in range(NT):
                            for kk in range(4):
                                k = hh * 4 + kk
                                ti = t.matmul(pt4[tt][:, :],
                                              xnT[:, k * TP + tt * 128:k * TP + (tt + 1) * 128],
                                              warena[sl][:, kk * 512:(kk + 1) * 512],
                                              start=(k == 0), stop=(k == 15),
                                              skip_group_check=True)
                        s_pe.inc(ti)
                mm_g = s_pe.n

                for tt in range(NT):
                    u = jc * NT + tt
                    sl = tt % 2
                    @block.scalar
                    def _(sc):
                        if tt == 0:
                            sc.wait_ge(s_pe.h, mm_g)
                        if u >= 2:
                            sc.wait_ge(s_gs2[sl].h, spill_g[u - 2])
                        gstag_g[u] = s_ac.inc(sc.activation(gstag[sl], pt4[tt][:, :], AF.Silu))

                    @block.sync
                    def _(sync):
                        sync.wait_ge(s_ac.h, gstag_g[u])
                        spill_g[u] = s_gs2[sl].inc(
                            sync.dma_start(out=gate_sp[tt * 128:(tt + 1) * 128, jc * 512:(jc + 1) * 512],
                                           in_=gstag[sl]), 16)
                evac_g[jc] = s_ac.n
            gate_spilled = (s_gs2[0].n, s_gs2[1].n)
            tap("t_gate", gate_sp)

            pe_b_x = s_pe.n
            wmark_x = {}
            @block.scalar
            def _(sc):
                sc.wait_ge(s_pe.h, pe_b_x)
                for jt in range(35):
                    sl = jt % 3
                    if p == 0:
                        wait_cast(sc, ("x", jt))
                    if jt >= 3:
                        sc.wait_ge(s_pe.h, pe_b_x + jt - 2)
                    wid = 128 if jt < 34 else 64
                    wmark_x[jt] = (sl, s_w3[sl].inc(
                        sc.dma_start(out=warena[sl][:, 0:KD * wid],
                                     in_=winx16[jt:jt + 1, :, 0:KD * wid]), 16))

            for jt in range(35):
                wid = 128 if jt < 34 else 64
                dstp = (pA[:, 0:512] if jt % 2 == 0 else pA[:, 512:1024]) if jt < 34 else p5[:, :]
                @block.tensor
                def _(t):
                    sl, wv = wmark_x[jt]
                    t.wait_ge(s_w3[sl].h, wv)
                    if jt >= 2:
                        t.wait_ge(s_ve.h, evac_x[jt - 2])
                    ti = None
                    for k in range(KD):
                        o = dstp[0:wid, :] if wid == 64 else dstp
                        ti = t.matmul(o, warena[sl][:, k * wid:(k + 1) * wid],
                                      xnT[:, k * TP:(k + 1) * TP],
                                      start=(k == 0), stop=(k == KD - 1))
                    s_pe.inc(ti)
                mm_x = s_pe.n

                @block.vector
                def _(v):
                    v.wait_ge(s_pe.h, mm_x)
                    if jt < 34:
                        evac_x[jt] = s_ve.inc(
                            v.tensor_copy(convin[:, jt * CW + 3:jt * CW + 3 + 512], dstp))
                    else:
                        evac_x[jt] = s_ve.inc(v.tensor_copy(dt_raw[:, :], dstp[0:NH, :]))
            inproj_ve = s_ve.n
            pe_after_x = s_pe.n
            tap("t_dtraw", dt_raw[:, :])
            tap("t_convin", convin)

            if STAGE < 2:
                _store(block, (s_ve,), s_out, x_res, out, bi, r0)
                continue

            tailv = r3(convin, CW)[:, :, 512:515]
            halov = r3(convin, CW)[:, :, 0:3]
            @block.vector
            def _(v):
                if first:
                    v.memset(halov, 0.0)
                else:
                    v.tensor_copy(halov, r3(tail[:, :], 3))
                s_ve.inc(v.tensor_copy(r3(tail[:, :], 3), tailv))
            conv_rdy = s_ve.n

            pe_b_c = s_pe.n
            @block.vector
            def _(v):
                for j in range(NCT):
                    for k in range(4):
                        idx = j * 4 + k
                        if j >= 2 and k == 0:
                            v.wait_ge(s_pe.h, pe_b_c + j - 1)
                        diag_m[idx] = s_ve.inc(
                            v.tensor_scalar(diagw[idx % 8], identb[:, :],
                                            convw[:, j * 4 + k:j * 4 + k + 1], None, ALU.mult))

            for j in range(NCT):
                @block.tensor
                def _(t):
                    if j == 0:
                        t.wait_ge(s_ve.h, conv_rdy)
                    if j >= 2:
                        t.wait_ge(s_ac.h, convev[j - 2])
                    ti = None
                    for k in range(4):
                        idx = j * 4 + k
                        t.wait_ge(s_ve.h, diag_m[idx])
                        ti = t.matmul(pt4[j % 2][:, :], diagw[idx % 8],
                                      convin[:, j * CW + k:j * CW + k + 512],
                                      start=(k == 0), stop=(k == 3))
                    s_pe.inc(ti)
                mm_c = s_pe.n

                @block.scalar
                def _(sc):
                    sc.wait_ge(s_pe.h, mm_c)
                    convev[j] = s_ac.inc(
                        sc.activation(convin[:, j * CW + 3:j * CW + 3 + 512],
                                      pt4[j % 2][:, :], AF.Silu,
                                      bias=convb[:, j:j + 1], scale=1.0))
            conv_ac = s_ac.n
            tap("t_convout", convin)

            if STAGE < 3:
                _store(block, (s_ve, s_ac), s_out, x_res, out, bi, r0)
                continue

            @block.vector
            def _(v):
                if "t_dtraw" in taps or "t_convin" in taps or "t_gate" in taps:
                    v.wait_ge(s_out.h, s_out.n)
                s_ve.inc(v.tensor_scalar(dt_raw[:, :], dt_raw[:, :], dtb[:, :], None, ALU.add))
            m1 = s_ve.n
            @block.scalar
            def _(sc):
                sc.wait_ge(s_ve.h, m1)
                sc.activation(sp_scr2[:, :], dt_raw[:, :], AF.Abs)
                sc.activation(sp_scr2[:, :], sp_scr2[:, :], AF.Exp, scale=-1.0)
                s_ac.inc(sc.activation(sp_scr2[:, :], sp_scr2[:, :], AF.Ln, bias=1.0))
            m2 = s_ac.n
            @block.vector
            def _(v):
                v.wait_ge(s_ac.h, m2)
                v.tensor_scalar(dt_raw[:, :], dt_raw[:, :], 0.0, None, ALU.max)
                v.tensor_tensor(dt_sp[:, :], dt_raw[:, :], sp_scr2[:, :], ALU.add)
                v.memset(zer[:, :], 0.0)
                s_ve.inc(v.tensor_scalar(a_dt[:, :], dt_sp[:, :], acol[:, :], None, ALU.mult))
            adt_ve = s_ve.n
            tap("t_dtsp", dt_sp[:, :])

            @block.sync
            def _(sync):
                sync.wait_ge(s_gs2[0].h, gate_spilled[0])
                sync.wait_ge(s_gs2[1].h, gate_spilled[1])
                sync.wait_ge(s_pe.h, last_yn_pe.get(0, 0))
                gate_pref[0] = s_gpre.inc(sync.dma_start(out=gate_ch[0], in_=gate_sp[0:128, :]), 16)

            for c in range(4):
                cb = 3 + c * 128

                @block.vector
                def _(v):
                    if c == 0:
                        v.wait_ge(s_ve.h, adt_ve)
                        if first:
                            v.memset(s_state[:, :], 0.0)
                    else:
                        v.wait_ge(s_sp.h, s_sp.n)
                    v.tensor_tensor_scan(acsT[:, :], a_dt[:, c * 128:(c + 1) * 128],
                                         zer, 0.0, ALU.add, ALU.add)
                    v.tensor_copy(hilo[:, 0:128], acsT[:, :])
                    v.tensor_tensor(hilo[:, 128:256], acsT[:, :], hilo[:, 0:128], ALU.subtract)
                    s_ve.inc(v.tensor_scalar(wdecT[:, :], acsT[:, :], acsT[:, 127:128], -1.0,
                                             ALU.subtract, ALU.mult))
                ach1 = s_ve.n
                @block.scalar
                def _(sc):
                    sc.wait_ge(s_ve.h, ach1)
                    s_ac.inc(sc.activation(wdecT[:, :], wdecT[:, :], AF.Exp))
                wdec_ac = s_ac.n
                @block.vector
                def _(v):
                    v.wait_ge(s_ac.h, wdec_ac)
                    s_ve.inc(v.tensor_tensor(dtwT[:, :], wdecT[:, :],
                                             dt_sp[:, c * 128:(c + 1) * 128], ALU.mult))
                dtw_ve = s_ve.n

                @block.gpsimd
                def _(g):
                    g.wait_ge(s_ve.h, ach1)
                    g.wait_ge(s_pe.h, prev_flat_pe)
                    s_sp.inc(g.dma_start(out=flat[0:1, :], in_=hilo[:, 0:128]), 16)
                    s_sp.inc(g.dma_start(out=flat[1:2, :], in_=hilo[:, 128:256]), 16)
                flat_sp = s_sp.n

                @block.tensor
                def _(t):
                    if c == 0:
                        t.wait_ge(s_ac.h, conv_ac)
                    t.wait_ge(s_ve.h, dtw_ve)
                    t.wait_ge(s_ac.h, prev_prep_ac)
                    t.wait_ge(s_ve.h, prev_ynT_ve)
                    t.transpose(p5[:, 0:64], dt_sp[:, c * 128:(c + 1) * 128], identf[0:64, 0:64])
                    t.transpose(p5[:, 64:128], acsT[:, :], identf[0:64, 0:64])
                    t.transpose(p5[:, 128:192], dtwT[:, :], identf[0:64, 0:64])
                    t.matmul(p5[:, 192:320], convin[:, 32 * CW + cb:32 * CW + cb + 128],
                             convin[:, 33 * CW + cb:33 * CW + cb + 128], start=True, stop=True)
                    t.wait_ge(s_sp.h, flat_sp)
                    fv = flat.rearrange("c (h t) -> c h t", h=NH)[:, :, 127]
                    t.matmul(p5[:, 320:384], ones2[:, :], fv, start=True, stop=True)
                    s_pe.inc(t.transpose(pb16[:, 0:128],
                                         convin[:, 32 * CW + cb:32 * CW + cb + 128], identb[:, :]))
                prep_pe = s_pe.n

                @block.vector
                def _(v):
                    v.wait_ge(s_pe.h, prep_pe)
                    v.tensor_copy(stok[:, :], p5[:, 0:192])
                    v.tensor_tensor(gm[:, :], p5[:, 192:320], triu01[:, :], ALU.mult)
                    s_ve.inc(v.tensor_copy(b_tok[:, :], pb16[:, 0:128]))
                prep_ve = s_ve.n

                @block.scalar
                def _(sc):
                    sc.wait_ge(s_ve.h, prep_ve)
                    sc.activation(expa[:, :], stok[:, 64:128], AF.Exp)
                    s_ac.inc(sc.activation(decr[:, :], p5[:, 320:384], AF.Exp))
                prep_ac = s_ac.n
                prev_prep_ac = prep_ac

                pe_b_xm = s_pe.n
                ve_b_xm = s_ve.n
                @block.tensor
                def _(t):
                    t.wait_ge(s_ve.h, prep_ve)
                    t.wait_ge(s_ac.h, prep_ac)
                    for r in range(8):
                        if r >= 2:
                            t.wait_ge(s_ve.h, ve_b_xm + r - 1)
                        dstp = pxb[r % 2]
                        ti = None
                        for kk in range(4):
                            j = r * 4 + kk
                            ti = t.transpose(dstp[:, kk * 128:(kk + 1) * 128],
                                             convin[:, j * CW + cb:j * CW + cb + 128],
                                             identb[:, :])
                        s_pe.inc(ti)
                @block.vector
                def _(v):
                    for r in range(8):
                        v.wait_ge(s_pe.h, pe_b_xm + r + 1)
                        s_ve.inc(v.tensor_copy(x_m[:, r * 512:(r + 1) * 512], pxb[r % 2]))
                assert s_ve.n == ve_b_xm + 8
                xm_ve = s_ve.n

                if c == 0:
                    tap("t_xmtok", x_m)
                    tap("t_acs", acsT[:, :])

                for g8 in range(8):
                    @block.vector
                    def _(v):
                        if c == 0 and g8 == 0:
                            v.wait_ge(s_gs2[0].h, gate_spilled[0])
                            v.wait_ge(s_gs2[1].h, gate_spilled[1])
                        if g8 >= 2:
                            v.wait_ge(s_pe.h, s16_cons[g8 - 2])
                        s16_m[g8] = s_ve.inc(v.tensor_copy(s16g[g8 % 2],
                                                           s_state[:, g8 * 512:(g8 + 1) * 512]))

                    @block.gpsimd
                    def _(g):
                        g.wait_ge(s_ve.h, xm_ve)
                        if c == 0 and g8 == 0:
                            g.wait_ge(s_gs2[0].h, gate_spilled[0])
                            g.wait_ge(s_gs2[1].h, gate_spilled[1])
                        if g8 >= 2:
                            g.wait_ge(s_pe.h, mt_cons[(g8 - 2) * 8 + 7])
                        dv = dh_rep[:, g8 * 8:(g8 + 1) * 8].unsqueeze(2).broadcast_to((128, 8, 64))
                        s_gp.inc(g.tensor_tensor(r3(dres[g8 % 2], 64),
                                                 r3(x_m[:, g8 * 512:(g8 + 1) * 512], 64), dv, ALU.mult))
                        g.wait_ge(s_ve.h, prep_ve)
                        wv = r3(stok[:, 128:192], 1)[:, g8 * 8:(g8 + 1) * 8, :].broadcast_to((128, 8, 64))
                        s_gp.inc(g.tensor_tensor(r3(xcw[g8 % 2], 64),
                                                 r3(x_m[:, g8 * 512:(g8 + 1) * 512], 64), wv, ALU.mult))
                        g.wait_ge(s_ac.h, prep_ac)
                        if g8 >= 1:
                            g.wait_ge(s_ve.h, supd[g8 - 1])
                        dcv = decr[:, g8 * 8:(g8 + 1) * 8].unsqueeze(2).broadcast_to((128, 8, 64))
                        s_gp.inc(g.tensor_tensor(r3(sdec[g8 % 2], 64),
                                                 r3(s_state[:, g8 * 512:(g8 + 1) * 512], 64), dcv, ALU.mult))
                    pool_gp = s_gp.n

                    @block.tensor
                    def _(t):
                        if g8 == 0:
                            t.wait_ge(s_sp.h, flat_sp)
                        if g8 >= 1:
                            t.wait_ge(s_ve.h, dneg_m[g8 - 1])
                        t.matmul(pA[:, 0:512], ones2[:, :], flat[:, g8 * 1024:g8 * 1024 + 512],
                                 start=True, stop=True)
                        s_pe.inc(t.matmul(pA[:, 512:1024], ones2[:, :],
                                          flat[:, g8 * 1024 + 512:(g8 + 1) * 1024],
                                          start=True, stop=True))
                    arep_pe = s_pe.n
                    if g8 == 7:
                        prev_flat_pe = arep_pe

                    @block.vector
                    def _(v):
                        v.wait_ge(s_pe.h, arep_pe)
                        if g8 >= 2:
                            v.wait_ge(s_ac.h, e_m[g8 - 2])
                        av = r3(stok[:, 64:128], 1)[:, g8 * 8:(g8 + 1) * 8, :].broadcast_to((128, 8, 128))
                        v.tensor_tensor(r3(dnegb[g8 % 2], 128), r3(pA[:, :], 128), av, ALU.subtract)
                        dneg_m[g8] = s_ve.inc(v.tensor_scalar(dnegb[g8 % 2], dnegb[g8 % 2],
                                                              0.0, None, ALU.min))

                    @block.scalar
                    def _(sc):
                        sc.wait_ge(s_ve.h, dneg_m[g8])
                        if g8 >= 2:
                            sc.wait_ge(s_ve.h, mt_cons_ve[(g8 - 2) * 8 + 7])
                        e_m[g8] = s_ac.inc(sc.activation(e_sb[g8 % 2], dnegb[g8 % 2], AF.Exp))

                    @block.tensor
                    def _(t):
                        t.wait_ge(s_ve.h, s16_m[g8])
                        if g8 >= 1:
                            t.wait_ge(s_ve.h, yevac[g8 - 1])
                        t.matmul(p2[:, :], convin[:, 33 * CW + cb:33 * CW + cb + 128],
                                 s16g[g8 % 2], start=True, stop=True)
                        t.wait_ge(s_gp.h, pool_gp)
                        s_pe.inc(t.matmul(p1[:, :], identb[:, :], dres[g8 % 2],
                                          start=True, stop=False, skip_group_check=True))
                    yoff_pe = s_pe.n
                    s16_cons[g8] = yoff_pe

                    for hh in range(8):
                        u = g8 * 8 + hh
                        @block.vector
                        def _(v):
                            v.wait_ge(s_ac.h, e_m[g8])
                            if u >= 2:
                                v.wait_ge(s_pe.h, mt_cons[u - 2])
                            mt_cons_ve[u] = s_ve.inc(
                                v.scalar_tensor_tensor(mt[u % 2], gm[:, :],
                                                       stok[:, u:u + 1],
                                                       e_sb[g8 % 2][:, hh * 128:(hh + 1) * 128],
                                                       ALU.mult, ALU.mult))
                        @block.tensor
                        def _(t):
                            t.wait_ge(s_ve.h, mt_cons_ve[u])
                            mt_cons[u] = s_pe.inc(
                                t.matmul(p1[:, hh * 64:(hh + 1) * 64], mt[u % 2],
                                         x_m[:, u * 64:(u + 1) * 64],
                                         start=False, stop=(hh == 7), skip_group_check=True))
                    ydiag_pe = s_pe.n

                    @block.tensor
                    def _(t):
                        if g8 >= 1:
                            t.wait_ge(s_ve.h, supd[g8 - 1])
                        s_pe.inc(t.matmul(p3[:, :], b_tok[:, :], xcw[g8 % 2],
                                          start=True, stop=True))
                    st_pe = s_pe.n

                    @block.vector
                    def _(v):
                        v.wait_ge(s_pe.h, yoff_pe)
                        if g8 == 0:
                            v.wait_ge(s_ac.h, prep_ac)
                        ev = expa[:, g8 * 8:(g8 + 1) * 8].unsqueeze(2).broadcast_to((128, 8, 64))
                        v.tensor_tensor(r3(tmpg, 64), r3(p2[:, :], 64), ev, ALU.mult)
                        v.wait_ge(s_pe.h, ydiag_pe)
                        yevac[g8] = s_ve.inc(
                            v.tensor_tensor(y_tok[:, g8 * 512:(g8 + 1) * 512],
                                            tmpg, p1[:, :], ALU.add))
                        v.wait_ge(s_pe.h, st_pe)
                        supd[g8] = s_ve.inc(
                            v.tensor_tensor(s_state[:, g8 * 512:(g8 + 1) * 512],
                                            sdec[g8 % 2], p3[:, :], ALU.add))
                y_ve = s_ve.n
                if c == 0:
                    tap("t_y", y_tok)
                if c == 3:
                    tap("t_S", s_state[:, :])

                if c < 3:
                    @block.sync
                    def _(sync):
                        sync.wait_ge(s_ve.h, y_ve)
                        sync.wait_ge(s_pe.h, last_yn_pe.get((c + 1) % 2, 0))
                        gate_pref[c + 1] = s_gpre.inc(
                            sync.dma_start(out=gate_ch[(c + 1) % 2],
                                           in_=gate_sp[(c + 1) * 128:(c + 2) * 128, :]), 16)

                @block.vector
                def _(v):
                    v.wait_ge(s_gpre.h, gate_pref[c])
                    v.wait_ge(s_ve.h, y_ve)
                    s_ve.inc(v.tensor_tensor(y_tok, y_tok, gate_ch[c % 2], ALU.mult))
                    for s in range(8):
                        v.bn_stats(bn6[:, s * 6:(s + 1) * 6], y_tok[:, s * 512:(s + 1) * 512])
                    v.tensor_copy(spacr[:, 0:32], identf[:, 0:32])
                    v.tensor_copy(spacr[:, 0:32], identf[:, 32:64])
                    v.tensor_copy(spacr[:, 0:32], identf[:, 64:96])
                    v.bn_aggr(tiny[:, 0:2], bn6[:, 0:48])
                    v.tensor_copy(spacr[:, 0:32], identf[:, 0:32])
                    v.tensor_copy(spacr[:, 0:32], identf[:, 32:64])
                    v.tensor_copy(spacr[:, 0:32], identf[:, 64:96])
                    v.scalar_tensor_tensor(tiny[:, 3:4], tiny[:, 0:1], tiny[:, 0:1],
                                           tiny[:, 1:2], ALU.mult, ALU.add)
                    v.tensor_copy(spacr[:, 0:32], identf[:, 0:32])
                    v.tensor_copy(spacr[:, 0:32], identf[:, 32:64])
                    s_ve.inc(v.tensor_copy(spacr[:, 0:32], identf[:, 64:96]))
                rms_ve = s_ve.n
                @block.scalar
                def _(sc):
                    sc.wait_ge(s_ve.h, rms_ve)
                    sc.activation(tiny[:, 5:6], tiny[:, 3:4], AF.Ln, bias=epsc[:, :])
                    s_ac.inc(sc.activation(tiny[:, 5:6], tiny[:, 5:6], AF.Exp, scale=-0.5))
                rms_ac = s_ac.n
                if c == 3 and "t_dbg_rms" in taps:
                    @block.gpsimd
                    def _(g):
                        g.wait_ge(s_ac.h, rms_ac)
                        s_sp.inc(g.dma_start(out=taps["t_dbg_rms"][:, 0:8], in_=tiny[:, :]), 16)
                        s_sp.inc(g.dma_start(out=taps["t_dbg_rms"][:, 8:56], in_=bn6[:, 0:48]), 16)
                    @block.vector
                    def _(v):
                        v.wait_ge(s_sp.h, s_sp.n)
                @block.vector
                def _(v):
                    v.wait_ge(s_ac.h, rms_ac)
                    if c > 0:
                        v.wait_ge(s_ynsp.h, ynsp[c - 1])
                    s_ve.inc(v.scalar_tensor_tensor(gate_ch[c % 2], y_tok, tiny[:, 5:6],
                                                    nw_rep[:, :], ALU.mult, ALU.mult))
                yn_ve = s_ve.n
                if c == 3:
                    tap("t_yn", gate_ch[c % 2])

                pe_b_yn = s_pe.n
                ve_b_yn = s_ve.n
                @block.tensor
                def _(t):
                    t.wait_ge(s_ve.h, yn_ve)
                    for r in range(8):
                        if r >= 2:
                            t.wait_ge(s_ve.h, ve_b_yn + r - 1)
                        dstp = pxb[r % 2]
                        ti = None
                        for kk in range(4):
                            k = r * 4 + kk
                            ti = t.transpose(dstp[:, kk * 128:(kk + 1) * 128],
                                             gate_ch[c % 2][:, k * 128:(k + 1) * 128], identb[:, :])
                        s_pe.inc(ti)
                @block.vector
                def _(v):
                    for r in range(8):
                        v.wait_ge(s_pe.h, pe_b_yn + r + 1)
                        s_ve.inc(v.tensor_copy(ynT_ch[:, r * 512:(r + 1) * 512], pxb[r % 2]))
                ynT_ve = s_ve.n
                prev_ynT_ve = ynT_ve
                last_yn_pe[c % 2] = s_pe.n
                @block.sync
                def _(sync):
                    sync.wait_ge(s_ve.h, ynT_ve)
                    ynsp[c] = s_ynsp.inc(
                        sync.dma_start(out=ynT_sp3[c:c + 1], in_=r3(ynT_ch, 128)), 16)
            ynT_spld = s_ynsp.n
            tap("t_ynT", ynT_sp3)

            if p == 0 and wout_cast:
                emit_cast(wout_cast)
                wout_cast = []

            if STAGE < 4:
                _store(block, (s_ve, s_ac), s_out, x_res, out, bi, r0)
                continue

            pbanks = [pA[:, 0:512], pA[:, 512:1024], p1[:, :], p2[:, :], p3[:, :],
                      p4[:, :], p5[:, :], pb16[:, :].bitcast(F32)]
            opv = {}
            opv_ac = {}
            for half in range(2):
                pe_b_o = s_pe.n
                ymark = {}
                wmark_o = {}
                @block.sync
                def _(sync):
                    sync.wait_ge(s_ynsp.h, ynT_spld)
                    sync.wait_ge(s_pe.h, s_pe.n)
                    for kt in range(32):
                        sl = kt % 4
                        if kt >= 4:
                            sync.wait_ge(s_pe.h, pe_b_o + kt - 3)
                        ymark[kt] = s_y4[sl].inc(sync.dma_start(
                            out=ynstr[sl],
                            in_=ynT_sp3.rearrange("c p k t -> p c k t")[:, :, kt:kt + 1, :]), 16)
                ldq = block.scalar if p == 0 else block.gpsimd

                @ldq
                def _(sc):
                    sc.wait_ge(s_pe.h, pe_after_x if half == 0 else pe_b_o)
                    for kt in range(32):
                        sl = kt % 3
                        if p == 0:
                            wait_cast(sc, ("wo", kt))
                        if kt >= 3:
                            sc.wait_ge(s_pe.h, pe_b_o + kt - 2)
                        src = wtile3(woutT16, kt * 128, 1, half * 1024, 1024)
                        wmark_o[kt] = s_w3[sl].inc(
                            sc.dma_start(out=r3(warena[sl][:, 0:1024], 1024), in_=src), 16)

                @block.tensor
                def _(t):
                    t.wait_ge(s_ve.h, s_ve.n)
                    t.wait_ge(s_ac.h, s_ac.n)
                    for kt in range(32):
                        t.wait_ge(s_y4[kt % 4].h, ymark[kt])
                        t.wait_ge(s_w3[kt % 3].h, wmark_o[kt])
                        ti = None
                        for tt in range(NT):
                            for db in range(2):
                                ti = t.matmul(pbanks[tt * 2 + db],
                                              ynstr[kt % 4][:, tt * 128:(tt + 1) * 128],
                                              warena[kt % 3][:, db * 512:(db + 1) * 512],
                                              start=(kt == 0), stop=(kt == 31),
                                              skip_group_check=True)
                        s_pe.inc(ti)
                op_mm = s_pe.n

                for b in range(8):
                    tt, db = b // 2, b % 2
                    dst = attn_tok[:, tt * D + half * 1024 + db * 512:
                                   tt * D + half * 1024 + (db + 1) * 512]
                    if b % 2 == 0:
                        @block.vector
                        def _(v):
                            v.wait_ge(s_pe.h, op_mm)
                            s_ve.inc(v.tensor_copy(dst, pbanks[b]))
                    else:
                        @block.scalar
                        def _(sc):
                            sc.wait_ge(s_pe.h, op_mm)
                            s_ac.inc(sc.activation(dst, pbanks[b], AF.Copy))
                opv[half] = s_ve.n
                opv_ac[half] = s_ac.n
            tap("t_attn", attn_tok)

            if p == 0 and late_cast:
                emit_cast(late_cast)
                late_cast = []

            @block.vector
            def _(v):
                v.wait_ge(s_ac.h, opv_ac[1])
            ln_stats(attn_tok)
            ln2_ac = s_ac.n
            @block.vector
            def _(v):
                v.wait_ge(s_ac.h, ln2_ac)
                for i in range(NT):
                    v.tensor_scalar(attn_tok[:, i * D:(i + 1) * D], attn_tok[:, i * D:(i + 1) * D],
                                    mv[:, 2 * i:2 * i + 1], rstd[:, i:i + 1],
                                    ALU.subtract, ALU.mult)
                    s_ve.inc(v.tensor_tensor(x_res[:, i * D:(i + 1) * D],
                                             x_res[:, i * D:(i + 1) * D],
                                             attn_tok[:, i * D:(i + 1) * D], ALU.add))
            tap("t_x1", x_res[:, :])

            if STAGE < 5:
                _store(block, (s_ve,), s_out, x_res, out, bi, r0)
                continue

            @block.vector
            def _(v):
                v.wait_ge(s_ld.h, ynT_spld)
            ln_stats(x_res)
            @block.tensor
            def _(t):
                t.wait_ge(s_ve.h, s_ve.n)
            ln_apply_transpose(x_res)
            xn2_ve = s_ve.n

            pe_b_f = s_pe.n
            wmark_f = {}
            @block.sync
            def _(sc):
                sc.wait_ge(s_pe.h, pe_b_f)
                for j in range(NJT):
                    sl0 = (2 * j) % 3
                    sl1 = (2 * j + 1) % 3
                    if p == 0:
                        wait_cast(sc, ("fg", j))
                        wait_cast(sc, ("fu", j))
                    if j >= 2:
                        sc.wait_ge(s_pe.h, pe_b_f + j - 1)
                    wmark_f[2 * j] = s_w3[sl0].inc(sc.dma_start(out=warena[sl0][:, :],
                                                   in_=wgs16[j:j + 1]), 16)
                    if j >= 1:
                        sc.wait_ge(s_pe.h, pe_b_f + j)
                    wmark_f[2 * j + 1] = s_w3[sl1].inc(sc.dma_start(out=warena[sl1][:, :],
                                                       in_=wus16[j:j + 1]), 16)

            for j in range(NJT):
                pg = p1 if j % 2 == 0 else p2
                pu = p3 if j % 2 == 0 else p4
                @block.tensor
                def _(t):
                    t.wait_ge(s_w3[(2 * j) % 3].h, wmark_f[2 * j])
                    t.wait_ge(s_w3[(2 * j + 1) % 3].h, wmark_f[2 * j + 1])
                    if j < 2:
                        t.wait_ge(s_ve.h, xn2_ve)
                    else:
                        t.wait_ge(s_ve.h, h_m[j - 2])
                        t.wait_ge(s_ac.h, sg_m[j - 2])
                    for k in range(KD):
                        t.matmul(pg[:, :], warena[(2 * j) % 3][:, k * 128:(k + 1) * 128],
                                 xnT[:, k * TP:(k + 1) * TP], start=(k == 0), stop=(k == KD - 1))
                    ti = None
                    for k in range(KD):
                        ti = t.matmul(pu[:, :], warena[(2 * j + 1) % 3][:, k * 128:(k + 1) * 128],
                                      xnT[:, k * TP:(k + 1) * TP], start=(k == 0), stop=(k == KD - 1))
                    s_pe.inc(ti)
                fmm = s_pe.n
                if j == 0 and "t_dbg_g" in taps:
                    @block.vector
                    def _(v):
                        v.wait_ge(s_pe.h, fmm)
                        s_ve.inc(v.tensor_copy(attn_tok[:, 0:512], pg[:, :]))
                        s_ve.inc(v.tensor_copy(attn_tok[:, 512:1024], pu[:, :]))
                    dbgv = s_ve.n
                    @block.gpsimd
                    def _(sync):
                        sync.wait_ge(s_ve.h, dbgv)
                        s_sp.inc(sync.dma_start(out=taps["t_dbg_g"], in_=attn_tok[:, 0:512]), 16)
                        s_sp.inc(sync.dma_start(out=taps["t_dbg_u"], in_=attn_tok[:, 512:1024]), 16)
                    @block.scalar
                    def _(sc):
                        sc.wait_ge(s_sp.h, s_sp.n)
                @block.scalar
                def _(sc):
                    sc.wait_ge(s_pe.h, fmm)
                    sg_m[j] = s_ac.inc(sc.activation(sg_j[j % 2], pg[:, :], AF.Silu,
                                                     scale=gs_c[:, j:j + 1]))
                if j == 0 and "t_dbg_sg" in taps:
                    @block.gpsimd
                    def _(sync):
                        sync.wait_ge(s_ac.h, sg_m[0])
                        s_sp.inc(sync.dma_start(out=taps["t_dbg_sg"], in_=sg_j[0]), 16)
                    @block.vector
                    def _(v):
                        v.wait_ge(s_sp.h, s_sp.n)
                @block.vector
                def _(v):
                    v.wait_ge(s_ac.h, sg_m[j])
                    h_m[j] = s_ve.inc(
                        v.scalar_tensor_tensor(hT(j), pu[:, :],
                                               us_c[:, j:j + 1], sg_j[j % 2],
                                               ALU.mult, ALU.mult))
            h_ve = s_ve.n
            if "t_h" in taps:
                tap("t_h", convin_b[:, 0:32 * TP])

            pe_b_d = s_pe.n
            sg_last = sg_m[NJT - 1]
            wmark_d = {}
            @block.sync
            def _(sync):
                sync.wait_ge(s_pe.h, pe_b_d)
                s_dsr.inc(sync.dma_start(out=ds_rep, in_=dsr_in), 16)
            dsr_m = s_dsr.n
            @block.scalar
            def _(sc):
                sc.wait_ge(s_pe.h, pe_b_d)
                for idx in range(2 * NJT):
                    half, k = idx // NJT, idx % NJT
                    sl = idx % 3
                    if p == 0:
                        wait_cast(sc, ("wd", k))
                    if idx >= 3:
                        sc.wait_ge(s_pe.h, pe_b_d + idx - 2)
                    src = wtile3(wdT16, k * 128, 1, half * 1024, 1024)
                    wmark_d[idx] = s_w3[sl].inc(
                        sc.dma_start(out=r3(warena[sl][:, 0:1024], 1024), in_=src), 16)

            dnv = {}
            for half in range(2):
                @block.tensor
                def _(t):
                    if half == 0:
                        t.wait_ge(s_ve.h, h_ve)
                        t.wait_ge(s_ac.h, sg_last)
                    else:
                        t.wait_ge(s_ve.h, dnv[0])
                    for k in range(NJT):
                        idx = half * NJT + k
                        t.wait_ge(s_w3[idx % 3].h, wmark_d[idx])
                        ti = None
                        for tt in range(NT):
                            for db in range(2):
                                ti = t.matmul(pbanks[tt * 2 + db],
                                              hT(k)[:, tt * 128:(tt + 1) * 128],
                                              warena[idx % 3][:, db * 512:(db + 1) * 512],
                                              start=(k == 0), stop=(k == NJT - 1),
                                              skip_group_check=True)
                        s_pe.inc(ti)
                dmm = s_pe.n
                @block.vector
                def _(v):
                    v.wait_ge(s_pe.h, dmm)
                    if half == 0:
                        v.wait_ge(s_dsr.h, dsr_m)
                    for b in range(8):
                        tt, db = b // 2, b % 2
                        dst = attn_tok[:, tt * D + half * 1024 + db * 512:
                                       tt * D + half * 1024 + (db + 1) * 512]
                        s_ve.inc(v.tensor_tensor(dst, pbanks[b],
                                                 ds_rep[:, half * 1024 + db * 512:
                                                        half * 1024 + (db + 1) * 512],
                                                 ALU.mult))
                dnv[half] = s_ve.n

            ln_stats(attn_tok)
            ln4_ac = s_ac.n
            @block.vector
            def _(v):
                v.wait_ge(s_ac.h, ln4_ac)
                for i in range(NT):
                    v.tensor_scalar(attn_tok[:, i * D:(i + 1) * D], attn_tok[:, i * D:(i + 1) * D],
                                    mv[:, 2 * i:2 * i + 1], rstd[:, i:i + 1],
                                    ALU.subtract, ALU.mult)
                    s_ve.inc(v.tensor_tensor(x_res[:, i * D:(i + 1) * D],
                                             x_res[:, i * D:(i + 1) * D],
                                             attn_tok[:, i * D:(i + 1) * D], ALU.add))
            _store(block, (s_ve,), s_out, x_res, out, bi, r0)

        @block.sync
        def _(sync):
            sync.wait_ge(s_out.h, s_out.n)
            sync.wait_ge(s_ld.h, s_ld.n)
            sync.wait_ge(s_sp.h, s_sp.n)
            sync.wait_ge(s_gpre.h, s_gpre.n)
            sync.wait_ge(s_ynsp.h, s_ynsp.n)
            sync.wait_ge(s_dsr.h, s_dsr.n)
            for sm in s_w3 + s_y4 + s_gs2 + s_cl2 + s_cw2:
                sync.wait_ge(sm.h, sm.n)

    return nc, taps


def _store(block, sems, s_out, x_res, out, bi, r0):
    @block.sync
    def _(sync):
        for sm in sems:
            sync.wait_ge(sm.h, sm.n)
        for i in range(NT):
            s_out.inc(sync.dma_start(out=out[bi, r0 + i * 128:r0 + (i + 1) * 128, :],
                                     in_=x_res[:, i * D:(i + 1) * D]), 16)


def _layout_inputs(inputs):
    f32 = np.float32
    asC = np.ascontiguousarray
    w = {}
    Wi = np.asarray(inputs["L0_attn_in_proj_w"], f32)
    w["win_g"] = asC(Wi[0:DIN].reshape(8, 512, 4, 4, 128)
                     .transpose(0, 2, 4, 3, 1).reshape(32, 128, 2048))
    wx = Wi[DIN:DIN + CONVD].reshape(NCT, 128, KD, 128).transpose(0, 3, 2, 1)
    wdt = Wi[DIN + CONVD:].reshape(64, KD, 128).transpose(2, 1, 0).reshape(1, 128, KD * 64)
    wxa = np.zeros((35, 128, KD * 128), f32)
    wxa[0:NCT] = wx.reshape(NCT, 128, KD * 128)
    wxa[NCT, :, 0:KD * 64] = wdt[0]
    w["win_x"] = asC(wxa)
    w["woutT"] = asC(np.asarray(inputs["L0_attn_out_proj_w"], f32).T)
    w["wg_s"] = asC(np.asarray(inputs["L0_ffn_gate_w"], f32)
                    .reshape(NJT, 128, KD, 128).transpose(0, 3, 2, 1).reshape(NJT, 128, KD * 128))
    w["wu_s"] = asC(np.asarray(inputs["L0_ffn_up_w"], f32)
                    .reshape(NJT, 128, KD, 128).transpose(0, 3, 2, 1).reshape(NJT, 128, KD * 128))
    w["wdT"] = asC(np.asarray(inputs["L0_ffn_down_w"], f32).T)
    w["convw"] = asC(np.asarray(inputs["L0_attn_conv_w"], f32).reshape(NCT, 128, 4)
                     .transpose(1, 0, 2).reshape(128, NCT * 4))
    w["convb"] = asC(np.asarray(inputs["L0_attn_conv_b"], f32).reshape(NCT, 128).T)
    w["dtb"] = asC(np.asarray(inputs["L0_attn_dt_bias"], f32).reshape(NH, 1))
    w["alog"] = asC(np.asarray(inputs["L0_attn_A_log"], f32).reshape(NH, 1))
    w["dh_rep"] = asC(np.broadcast_to(np.asarray(inputs["L0_attn_D"], f32)[None, :], (128, NH)))
    w["nw_rep"] = asC(np.broadcast_to(np.asarray(inputs["L0_attn_norm_w"], f32)[None, :], (128, DIN)))
    w["gs"] = asC(np.asarray(inputs["L0_ffn_gate_s"], f32).reshape(NJT, 128).T)
    w["us"] = asC(np.asarray(inputs["L0_ffn_up_s"], f32).reshape(NJT, 128).T)
    w["ds_rep"] = asC(np.broadcast_to(np.asarray(inputs["L0_ffn_down_s"], f32)[None, :],
                                      (128, D)))
    return w


def kernel(**inputs):
    x = np.asarray(inputs["L0_attn_hidden_states"], np.float32)
    w = _layout_inputs(inputs)
    nc, _ = build_program()
    in_maps = []
    for c in range(8):
        m = dict(w)
        m["x"] = np.ascontiguousarray(x[2 * c:2 * c + 2])
        in_maps.append(m)
    res = run_bass_kernel_spmd(nc, in_maps, core_ids=list(range(8)))
    return np.concatenate([r["out"] for r in res.results], axis=0)


# version 09 (final)
# speedup vs baseline: 1.1091x; 1.0792x over previous
import numpy as np
import ml_dtypes
from contextlib import ExitStack
import concourse.bass as bass
import concourse.mybir as mybir
from concourse.bass_utils import run_bass_kernel_spmd

F32 = mybir.dt.float32
BF16 = mybir.dt.bfloat16
AF = mybir.ActivationFunctionType
ALU = mybir.AluOpType

D = 2048
KD = D // 128
DIN = 4096
CONVD = 4352
NCT = CONVD // 128
CW = 515
PROJ = 8512
NH = 64
INTER = 5632
NJT = INTER // 128
TP = 512
NT = TP // 128
EPS = 1e-5
CCH = 2128

N_PASS = 8
STAGE = 6
TAPS = ()


def build_program():
    nc = bass.Bass("TRN2", target_bir_lowering=False, detect_race_conditions=False)

    from concourse.hw_specs import get_activation_tables
    tabs = list(get_activation_tables(nc.m.arch).values())
    set_silu = next(i for i, fs in enumerate(tabs) if AF.Silu in fs)
    set_lnexp = next(i for i, fs in enumerate(tabs)
                     if AF.Ln in fs and AF.Exp in fs and AF.Abs in fs and AF.Copy in fs)

    def preload(sc, idx):
        sc.add_instruction(mybir.InstLoadActFuncSet(
            name=nc.get_next_instruction_name(), act_func_set_id=idx, ins=[], outs=[]))

    x_in = nc.dram_tensor("x", [2, 2048, D], F32, kind="ExternalInput").ap()
    wing = nc.dram_tensor("win_g", [32, 128, 4 * 512], F32, kind="ExternalInput").ap()
    winx = nc.dram_tensor("win_x", [35, 128, KD * 128], F32, kind="ExternalInput").ap()
    woutT = nc.dram_tensor("woutT", [DIN, D], F32, kind="ExternalInput").ap()
    wgs = nc.dram_tensor("wg_s", [NJT, 128, KD * 128], F32, kind="ExternalInput").ap()
    wus = nc.dram_tensor("wu_s", [NJT, 128, KD * 128], F32, kind="ExternalInput").ap()
    wdT = nc.dram_tensor("wdT", [INTER, D], F32, kind="ExternalInput").ap()
    convw_in = nc.dram_tensor("convw", [128, NCT * 4], F32, kind="ExternalInput").ap()
    convb_in = nc.dram_tensor("convb", [128, NCT], F32, kind="ExternalInput").ap()
    dtb_in = nc.dram_tensor("dtb", [NH, 1], F32, kind="ExternalInput").ap()
    alog_in = nc.dram_tensor("alog", [NH, 1], F32, kind="ExternalInput").ap()
    dh_rep_in = nc.dram_tensor("dh_rep", [128, NH], F32, kind="ExternalInput").ap()
    nw_rep_in = nc.dram_tensor("nw_rep", [128, DIN], F32, kind="ExternalInput").ap()
    gs_in = nc.dram_tensor("gs", [128, NJT], F32, kind="ExternalInput").ap()
    us_in = nc.dram_tensor("us", [128, NJT], F32, kind="ExternalInput").ap()
    dsr_in = nc.dram_tensor("ds_rep", [128, D], F32, kind="ExternalInput").ap()
    out = nc.dram_tensor("out", [2, 2048, D], F32, kind="ExternalOutput").ap()

    wing16 = nc.dram_tensor("win_g16", [32, 128, 4 * 512], BF16, kind="Internal").ap()
    winx16 = nc.dram_tensor("win_x16", [35, 128, KD * 128], BF16, kind="Internal").ap()
    woutT16 = nc.dram_tensor("woutT16", [DIN, D], BF16, kind="Internal").ap()
    wgs16 = nc.dram_tensor("wg_s16", [NJT, 128, KD * 128], BF16, kind="Internal").ap()
    wus16 = nc.dram_tensor("wu_s16", [NJT, 128, KD * 128], BF16, kind="Internal").ap()
    wdT16 = nc.dram_tensor("wdT16", [INTER, D], BF16, kind="Internal").ap()
    gate_sp = nc.dram_tensor("gate_sp", [TP, DIN], BF16, kind="Internal").ap()
    ynT_sp3 = nc.dram_tensor("ynT_sp", [4, 128, 32, 128], BF16, kind="Internal").ap()

    tap_specs = {
        "t_xn1T": ([128, KD * TP], BF16),
        "t_gate": ([TP, DIN], BF16),
        "t_convin": ([128, NCT * CW], BF16),
        "t_dtraw": ([NH, TP], F32),
        "t_dtsp": ([NH, TP], F32),
        "t_convout": ([128, NCT * CW], BF16),
        "t_acs": ([NH, 128], F32),
        "t_xmtok": ([128, DIN], BF16),
        "t_y": ([128, DIN], BF16),
        "t_yn": ([128, DIN], BF16),
        "t_ynT": ([DIN, TP], BF16),
        "t_attn": ([128, NT * D], F32),
        "t_x1": ([128, NT * D], F32),
        "t_h": ([128, 32 * TP], BF16),
        "t_dbg_sg": ([128, TP], BF16),
        "t_dbg_rms": ([128, 64], F32),
        "t_dbg_g": ([128, TP], F32),
        "t_dbg_u": ([128, TP], F32),
        "t_S": ([128, DIN], F32),
    }
    taps = {}
    for name in TAPS:
        shape, dt = tap_specs[name]
        taps[name] = nc.dram_tensor(name, shape, dt, kind="ExternalOutput").ap()

    es = ExitStack()
    sbuf = lambda n, s, d: es.enter_context(nc.sbuf_tensor(n, s, d))
    psal = lambda n, s, d: es.enter_context(nc.psum_tensor(n, s, d))

    with es:
        x_res = sbuf("x_res", [128, NT * D], F32)
        xn_tok_b = sbuf("xn_tok", [128, 2 * D], BF16)
        xn_tok = [xn_tok_b[:, i * D:(i + 1) * D] for i in range(2)]
        ynT_ch = xn_tok_b[:, 0:DIN]
        xnT_b = sbuf("xnT", [128, KD * TP], BF16)
        xnT = xnT_b[:, :]
        flat = xnT_b[0:2, :]
        convin_b = sbuf("convin", [128, NCT * CW], BF16)
        convin = convin_b[:, :]
        tail = sbuf("tail", [128, NCT * 3], BF16)
        dt_raw = sbuf("dt_raw", [NH, TP], F32)
        dt_sp = sbuf("dt_sp", [NH, TP], F32)
        sp_scr2 = sbuf("sp_scr2", [NH, TP], F32)
        a_dt = sp_scr2[:, :]
        acsT = sbuf("acsT", [NH, 128], F32)
        hilo = sbuf("hilo", [NH, 256], BF16)
        wdecT = sbuf("wdecT", [NH, 128], F32)
        dtwT = sbuf("dtwT", [NH, 128], F32)
        stok = sbuf("stok", [128, 192], F32)
        expa = sbuf("expa", [128, NH], F32)
        decr = sbuf("decr", [128, NH], F32)
        b_tok = sbuf("b_tok", [128, 128], BF16)
        gm = sbuf("gm", [128, 128], BF16)
        s_state = sbuf("s_state", [128, DIN], F32)

        A1 = sbuf("A1", [128, 16384], BF16)
        x_m = A1[:, 0:4096]
        y_tok = A1[:, 4096:8192]
        dnegb = [A1[:, 8192 + i * 2048: 8192 + (i + 1) * 2048].bitcast(F32) for i in range(2)]
        e_sb = [A1[:, 12288 + i * 1024: 12288 + (i + 1) * 1024] for i in range(2)]
        sdec = [A1[:, 14336 + i * 1024: 14336 + (i + 1) * 1024].bitcast(F32) for i in range(2)]
        attn_tok = A1[:, :].bitcast(F32)

        A2 = sbuf("A2", [128, 4608], BF16)
        mt = [A2[:, i * 128:(i + 1) * 128] for i in range(2)]
        dres = [A2[:, 256 + i * 512: 256 + (i + 1) * 512] for i in range(2)]
        xcw = [A2[:, 1280 + i * 512: 1280 + (i + 1) * 512] for i in range(2)]
        tmpg = A2[:, 2304:3328].bitcast(F32)
        s16g = [A2[:, 3328 + i * 512: 3328 + (i + 1) * 512] for i in range(2)]
        zerb = A2[:, 4352:4608].bitcast(F32)
        zer = zerb[0:NH, :]
        gstag = [A2[:, i * 512:(i + 1) * 512] for i in range(2)]
        diagw = [A2[:, 2048 + i * 128: 2048 + (i + 1) * 128] for i in range(8)]
        it1 = A2[:, 0:256].bitcast(F32)
        it2 = A2[:, 256:512].bitcast(F32)
        ynstr = [A2[:, i * 512:(i + 1) * 512] for i in range(4)]
        aostr = [A2[:, 2048 + i * 1024: 2048 + (i + 1) * 1024].bitcast(F32) for i in range(2)]
        sg_j = [A2[:, i * 512:(i + 1) * 512] for i in range(2)]

        gate_ch_b = sbuf("gate_ch", [128, 2 * DIN], BF16)
        gate_ch = [gate_ch_b[:, i * DIN:(i + 1) * DIN] for i in range(2)]

        def hT(j):
            if j < 32:
                return convin_b[:, j * TP:(j + 1) * TP]
            return gate_ch_b[:, (j - 32) * TP:(j - 31) * TP]

        bn6 = sbuf("bn6", [128, 96], F32)
        mv = sbuf("mv", [128, 2 * NT], F32)
        spacr = sbuf("spacr", [128, 32], F32)
        rstd = sbuf("rstd", [128, NT], F32)
        vtmp = sbuf("vtmp", [128, NT], F32)
        epsc = sbuf("epsc", [128, 1], F32)
        tiny = sbuf("tiny", [128, 8], F32)
        warena_b = sbuf("warena", [128, 3 * 2048], BF16)
        warena = [warena_b[:, i * 2048:(i + 1) * 2048] for i in range(3)]
        wcast_b = sbuf("wcast", [128, 2 * CCH], BF16)
        wcast = [wcast_b[:, i * CCH:(i + 1) * CCH] for i in range(2)]
        identb = sbuf("identb", [128, 128], BF16)
        identf = sbuf("identf", [128, 128], F32)
        triu01 = sbuf("triu01", [128, 128], BF16)
        ones2 = sbuf("ones2", [2, 128], BF16)
        convw = sbuf("convw_s", [128, NCT * 4], F32)
        convb = sbuf("convb_s", [128, NCT], F32)
        dtb = sbuf("dtb_s", [NH, 1], F32)
        acol = sbuf("acol", [NH, 1], F32)
        dh_rep = sbuf("dh_rep_s", [128, NH], F32)
        nw_rep = sbuf("nw_rep_s", [128, DIN], BF16)
        gs_c = sbuf("gs_c", [128, NJT], F32)
        us_c = sbuf("us_c", [128, NJT], F32)
        ds_rep = xnT_b[:, :].bitcast(F32)[:, 0:D]

        pA = psal("pA", [128, 1024], F32)
        p1 = psal("p1", [128, 512], F32)
        p2 = psal("p2", [128, 512], F32)
        p3 = psal("p3", [128, 512], F32)
        p4 = psal("p4", [128, 512], F32)
        p5 = psal("p5", [128, 512], F32)
        pb16 = psal("pb16", [128, 1024], BF16)
        pt4 = [p1, p2, p3, p4]
        pxb = [pb16[:, 0:512], p5[:, :].bitcast(BF16)[:, 0:512]]

        class Sem:
            def __init__(self, name):
                self.h = es.enter_context(nc.semaphore(name))
                self.n = 0

            def inc(self, instr, k=1):
                instr.then_inc(self.h, k)
                self.n += k
                return self.n

        s_ld = Sem("s_ld")
        s_w3 = [Sem(f"s_w{i}") for i in range(3)]
        s_y4 = [Sem(f"s_y{i}") for i in range(4)]
        s_gs2 = [Sem(f"s_gs{i}") for i in range(2)]
        s_gpre = Sem("s_gpre")
        s_ynsp = Sem("s_ynsp")
        s_dsr = Sem("s_dsr")
        s_sp = Sem("s_sp")
        s_cl2 = [Sem(f"s_cl{i}") for i in range(2)]
        s_cw2 = [Sem(f"s_cw{i}") for i in range(2)]
        s_pe = Sem("s_pe")
        s_ve = Sem("s_ve")
        s_ac = Sem("s_ac")
        s_gp = Sem("s_gp")
        s_out = Sem("s_out")

        block = es.enter_context(nc.Block())

        def r3(ap2d, inner):
            return ap2d.rearrange("p (a b) -> p a b", b=inner)

        def wtile3(wT, row0, nkt, col0, wid):
            v = wT.rearrange("(a p) j -> p a j", p=128)
            return v[:, row0 // 128: row0 // 128 + nkt, col0:col0 + wid]

        @block.sync
        def _(sync):
            for dst, src in [
                (convw[:, :], convw_in), (convb[:, :], convb_in),
                (dtb[:, :], dtb_in), (acol[:, :], alog_in),
                (dh_rep[:, :], dh_rep_in),
                (gs_c[:, :], gs_in), (us_c[:, :], us_in),
            ]:
                s_ld.inc(sync.dma_start(out=dst, in_=src), 16)
        consts_ld = s_ld.n

        @block.gpsimd
        def _(g):
            s_sp.inc(g.dma_start(out=nw_rep[:, :], in_=nw_rep_in), 16)
            g.iota(it1[:, :], pattern=[[1, 128]], base=0, channel_multiplier=0,
                   allow_small_or_imprecise_dtypes=True)
            s_gp.inc(g.iota(it2[:, :], pattern=[[0, 128]], base=0, channel_multiplier=1,
                            allow_small_or_imprecise_dtypes=True))

        @block.vector
        def _(v):
            v.wait_ge(s_gp.h, s_gp.n)
            v.tensor_tensor(identf[:, :], it1[:, :], it2[:, :], ALU.is_equal)
            v.tensor_copy(identb[:, :], identf[:, :])
            v.tensor_tensor(triu01[:, :], it1[:, :], it2[:, :], ALU.is_ge)
            v.memset(ones2[:, :], 1.0)
            v.memset(epsc[:, :], EPS)
            v.memset(s_state[:, :], 0.0)
            v.memset(convin_b[:, :], 0.0)
            s_ve.inc(v.memset(tail[:, :], 0.0))

        @block.scalar
        def _(sc):
            preload(sc, set_lnexp)
            sc.wait_ge(s_ld.h, consts_ld)
            s_ac.inc(sc.activation(acol[:, :], acol[:, :], AF.Exp))

        @block.vector
        def _(v):
            v.wait_ge(s_ac.h, s_ac.n)
            s_ve.inc(v.tensor_scalar(acol[:, :], acol[:, :], -1.0, None, ALU.mult))

        wready = {}
        cast_state = {"i": 0}

        def emit_cast(jobs):
            i0 = cast_state["i"]
            @block.gpsimd
            def _(g):
                for k, (key, src, dst) in enumerate(jobs):
                    i = i0 + k
                    sl = i % 2
                    w = src.shape[-1]
                    if i >= 2:
                        g.wait_ge(s_cw2[sl].h, s_cw2[sl].n)
                    s_cl2[sl].inc(g.dma_start(out=wcast[sl][:, 0:w], in_=src), 16)
                    g.wait_ge(s_cl2[sl].h, s_cl2[sl].n)
                    s_cw2[sl].inc(g.dma_start(out=dst, in_=wcast[sl][:, 0:w]), 16)
                    wready[key] = (sl, s_cw2[sl].n)
            cast_state["i"] = i0 + len(jobs)

        def wait_cast(eng, key):
            sl, cnt = wready[key]
            eng.wait_ge(s_cw2[sl].h, cnt)

        early_cast = []
        for s in range(32):
            early_cast.append((("g", s), wing[s:s + 1], wing16[s:s + 1]))
        for jt in range(34):
            early_cast.append((("x", jt), winx[jt:jt + 1], winx16[jt:jt + 1]))
        early_cast.append((("x", 34), winx[34:35, :, 0:KD * 64], winx16[34:35, :, 0:KD * 64]))
        emit_cast(early_cast)
        wout_cast = [(("wo", kt), woutT[kt * 128:(kt + 1) * 128, :],
                      woutT16[kt * 128:(kt + 1) * 128, :]) for kt in range(32)]
        late_cast = []
        for j in range(NJT):
            late_cast.append((("fg", j), wgs[j:j + 1], wgs16[j:j + 1]))
            late_cast.append((("fu", j), wus[j:j + 1], wus16[j:j + 1]))
        for k in range(NJT):
            late_cast.append((("wd", k), wdT[k * 128:(k + 1) * 128, :],
                              wdT16[k * 128:(k + 1) * 128, :]))

        def tap(name, src_ap):
            if name not in taps:
                return
            @block.sync
            def _(sync):
                sync.wait_ge(s_ve.h, s_ve.n)
                sync.wait_ge(s_ac.h, s_ac.n)
                sync.wait_ge(s_pe.h, s_pe.n)
                sync.wait_ge(s_gp.h, s_gp.n)
                sync.wait_ge(s_ld.h, s_ld.n)
                sync.wait_ge(s_sp.h, s_sp.n)
                s_out.inc(sync.dma_start(out=taps[name], in_=src_ap), 16)
            @block.vector
            def _(v):
                v.wait_ge(s_out.h, s_out.n)

        def ln_stats(src):
            nsub = D // 512
            @block.vector
            def _(v):
                for s in range(nsub):
                    for i in range(NT):
                        v.bn_stats(bn6[:, (i * nsub + s) * 6:(i * nsub + s + 1) * 6],
                                   src[:, i * D + s * 512:i * D + (s + 1) * 512])
                for i in range(NT):
                    v.bn_aggr(mv[:, 2 * i:2 * i + 2], bn6[:, i * nsub * 6:(i + 1) * nsub * 6])
                v.tensor_copy(spacr[:, 0:32], identf[:, 0:32])
                v.tensor_copy(spacr[:, 0:32], identf[:, 32:64])
                s_ve.inc(v.tensor_copy(spacr[:, 0:32], identf[:, 64:96]))

            @block.scalar
            def _(sc):
                sc.wait_ge(s_ve.h, s_ve.n)
                mvv = r3(mv[:, :], 2)[:, :, 1:2]
                sc.activation(r3(rstd[:, 0:NT], 1), mvv, AF.Ln, bias=epsc[:, :])
                s_ac.inc(sc.activation(rstd[:, 0:NT], rstd[:, 0:NT], AF.Exp, scale=-0.5))

        def ln_apply_transpose(src):
            stats_ac = s_ac.n
            pe_b = s_pe.n
            ve_b = s_ve.n

            def evac_mark(gr):
                return ve_b + 5 * (gr // 4) + 2 + (gr % 4)

            for i in range(NT):
                @block.vector
                def _(v):
                    v.wait_ge(s_ac.h, stats_ac)
                    if i >= 2:
                        v.wait_ge(s_pe.h, pe_b + 4 * (i - 1))
                    s_ve.inc(v.tensor_scalar(xn_tok[i % 2], src[:, i * D:(i + 1) * D],
                                             mv[:, 2 * i:2 * i + 1], rstd[:, i:i + 1],
                                             ALU.subtract, ALU.mult))

                @block.tensor
                def _(t):
                    t.wait_ge(s_ve.h, ve_b + 5 * i + 1)
                    for r in range(4):
                        gr = 4 * i + r
                        if gr >= 2:
                            t.wait_ge(s_ve.h, evac_mark(gr - 2))
                        dstp = pxb[gr % 2]
                        ti = None
                        for kk in range(4):
                            k = r * 4 + kk
                            ti = t.transpose(dstp[:, kk * 128:(kk + 1) * 128],
                                             xn_tok[i % 2][:, k * 128:(k + 1) * 128], identb[:, :])
                        s_pe.inc(ti)

                @block.vector
                def _(v):
                    for r in range(4):
                        gr = 4 * i + r
                        v.wait_ge(s_pe.h, pe_b + gr + 1)
                        dst = r3(xnT, TP)[:, r * 4:r * 4 + 4, i * 128:(i + 1) * 128]
                        s_ve.inc(v.tensor_copy(dst, r3(pxb[gr % 2], 128)))
                        assert s_ve.n == evac_mark(gr)
            assert s_pe.n == pe_b + 4 * NT and s_ve.n == ve_b + 5 * NT

        prev_flat_pe = 0
        prev_prep_ac = 0
        prev_ynT_ve = 0
        last_yn_pe = {}
        for p in range(N_PASS):
            bi = p // 4
            r0 = (p % 4) * TP
            first = (p % 4) == 0

            spill_g = {}
            gstag_g = {}
            evac_g = {}
            evac_x = {}
            diag_m = {}
            convev = {}
            gate_pref = {}
            ynsp = {}
            dneg_m = {}
            e_m = {}
            mt_cons = {}
            mt_cons_ve = {}
            yevac = {}
            supd = {}
            tr_m = {}
            h_m = {}
            sg_m = {}
            dv_m = {}
            dtr_m = {}
            s16_m = {}
            s16_cons = {}

            @block.sync
            def _(sync):
                if p > 0:
                    sync.wait_ge(s_out.h, s_out.n)
                for i in range(NT):
                    s_ld.inc(sync.dma_start(out=x_res[:, i * D:(i + 1) * D],
                                            in_=x_in[bi, r0 + i * 128:r0 + (i + 1) * 128, :]), 16)
            xload = s_ld.n

            @block.vector
            def _(v):
                v.wait_ge(s_ld.h, xload)
            ln_stats(x_res)
            @block.tensor
            def _(t):
                t.wait_ge(s_ve.h, s_ve.n)
            ln_apply_transpose(x_res)
            xnT_ve = s_ve.n
            tap("t_xn1T", xnT)

            pe_b_g = s_pe.n
            wmark_g = {}

            def load_gate_quarter(sc, idx):
                sl = idx % 3
                if p == 0:
                    wait_cast(sc, ("g", idx))
                if idx >= 3:
                    sc.wait_ge(s_pe.h, pe_b_g + idx - 2)
                wmark_g[idx] = (sl, s_w3[sl].inc(
                    sc.dma_start(out=warena[sl][:, :], in_=wing16[idx:idx + 1]), 16))

            @block.scalar
            def _(sc):
                preload(sc, set_silu)
                for idx in range(3):
                    load_gate_quarter(sc, idx)

            for jc in range(8):
                @block.scalar
                def _(sc):
                    for hh in range(4):
                        idx = jc * 4 + hh + 3
                        if idx < 32:
                            load_gate_quarter(sc, idx)

                @block.tensor
                def _(t):
                    if jc == 0:
                        t.wait_ge(s_ve.h, xnT_ve)
                    if jc >= 1:
                        t.wait_ge(s_ac.h, evac_g[jc - 1])
                    for hh in range(4):
                        idx = jc * 4 + hh
                        sl, wv = wmark_g[idx]
                        t.wait_ge(s_w3[sl].h, wv)
                        ti = None
                        for tt in range(NT):
                            for kk in range(4):
                                k = hh * 4 + kk
                                ti = t.matmul(pt4[tt][:, :],
                                              xnT[:, k * TP + tt * 128:k * TP + (tt + 1) * 128],
                                              warena[sl][:, kk * 512:(kk + 1) * 512],
                                              start=(k == 0), stop=(k == 15),
                                              skip_group_check=True)
                        s_pe.inc(ti)
                mm_g = s_pe.n

                for tt in range(NT):
                    u = jc * NT + tt
                    sl = tt % 2
                    @block.scalar
                    def _(sc):
                        if tt == 0:
                            sc.wait_ge(s_pe.h, mm_g)
                        if u >= 2:
                            sc.wait_ge(s_gs2[sl].h, spill_g[u - 2])
                        gstag_g[u] = s_ac.inc(sc.activation(gstag[sl], pt4[tt][:, :], AF.Silu))

                    @block.sync
                    def _(sync):
                        sync.wait_ge(s_ac.h, gstag_g[u])
                        spill_g[u] = s_gs2[sl].inc(
                            sync.dma_start(out=gate_sp[tt * 128:(tt + 1) * 128, jc * 512:(jc + 1) * 512],
                                           in_=gstag[sl]), 16)
                evac_g[jc] = s_ac.n
            gate_spilled = (s_gs2[0].n, s_gs2[1].n)
            tap("t_gate", gate_sp)

            pe_b_x = s_pe.n
            wmark_x = {}
            @block.scalar
            def _(sc):
                sc.wait_ge(s_pe.h, pe_b_x)
                for jt in range(35):
                    sl = jt % 3
                    if p == 0:
                        wait_cast(sc, ("x", jt))
                    if jt >= 3:
                        sc.wait_ge(s_pe.h, pe_b_x + jt - 2)
                    wid = 128 if jt < 34 else 64
                    wmark_x[jt] = (sl, s_w3[sl].inc(
                        sc.dma_start(out=warena[sl][:, 0:KD * wid],
                                     in_=winx16[jt:jt + 1, :, 0:KD * wid]), 16))

            for jt in range(35):
                wid = 128 if jt < 34 else 64
                dstp = (pA[:, 0:512] if jt % 2 == 0 else pA[:, 512:1024]) if jt < 34 else p5[:, :]
                @block.tensor
                def _(t):
                    sl, wv = wmark_x[jt]
                    t.wait_ge(s_w3[sl].h, wv)
                    if jt >= 2:
                        t.wait_ge(s_ve.h, evac_x[jt - 2])
                    ti = None
                    for k in range(KD):
                        o = dstp[0:wid, :] if wid == 64 else dstp
                        ti = t.matmul(o, warena[sl][:, k * wid:(k + 1) * wid],
                                      xnT[:, k * TP:(k + 1) * TP],
                                      start=(k == 0), stop=(k == KD - 1))
                    s_pe.inc(ti)
                mm_x = s_pe.n

                @block.vector
                def _(v):
                    v.wait_ge(s_pe.h, mm_x)
                    if jt < 34:
                        evac_x[jt] = s_ve.inc(
                            v.tensor_copy(convin[:, jt * CW + 3:jt * CW + 3 + 512], dstp))
                    else:
                        evac_x[jt] = s_ve.inc(v.tensor_copy(dt_raw[:, :], dstp[0:NH, :]))
            inproj_ve = s_ve.n
            pe_after_x = s_pe.n
            tap("t_dtraw", dt_raw[:, :])
            tap("t_convin", convin)

            if STAGE < 2:
                _store(block, (s_ve,), s_out, x_res, out, bi, r0)
                continue

            tailv = r3(convin, CW)[:, :, 512:515]
            halov = r3(convin, CW)[:, :, 0:3]
            @block.vector
            def _(v):
                if first:
                    v.memset(halov, 0.0)
                else:
                    v.tensor_copy(halov, r3(tail[:, :], 3))
                s_ve.inc(v.tensor_copy(r3(tail[:, :], 3), tailv))
            conv_rdy = s_ve.n

            pe_b_c = s_pe.n
            @block.vector
            def _(v):
                for j in range(NCT):
                    for k in range(4):
                        idx = j * 4 + k
                        if j >= 2 and k == 0:
                            v.wait_ge(s_pe.h, pe_b_c + j - 1)
                        diag_m[idx] = s_ve.inc(
                            v.tensor_scalar(diagw[idx % 8], identb[:, :],
                                            convw[:, j * 4 + k:j * 4 + k + 1], None, ALU.mult))

            for j in range(NCT):
                @block.tensor
                def _(t):
                    if j == 0:
                        t.wait_ge(s_ve.h, conv_rdy)
                    if j >= 2:
                        t.wait_ge(s_ac.h, convev[j - 2])
                    ti = None
                    for k in range(4):
                        idx = j * 4 + k
                        t.wait_ge(s_ve.h, diag_m[idx])
                        ti = t.matmul(pt4[j % 2][:, :], diagw[idx % 8],
                                      convin[:, j * CW + k:j * CW + k + 512],
                                      start=(k == 0), stop=(k == 3))
                    s_pe.inc(ti)
                mm_c = s_pe.n

                @block.scalar
                def _(sc):
                    sc.wait_ge(s_pe.h, mm_c)
                    convev[j] = s_ac.inc(
                        sc.activation(convin[:, j * CW + 3:j * CW + 3 + 512],
                                      pt4[j % 2][:, :], AF.Silu,
                                      bias=convb[:, j:j + 1], scale=1.0))
            conv_ac = s_ac.n
            tap("t_convout", convin)

            if STAGE < 3:
                _store(block, (s_ve, s_ac), s_out, x_res, out, bi, r0)
                continue

            @block.vector
            def _(v):
                if "t_dtraw" in taps or "t_convin" in taps or "t_gate" in taps:
                    v.wait_ge(s_out.h, s_out.n)
                s_ve.inc(v.tensor_scalar(dt_raw[:, :], dt_raw[:, :], dtb[:, :], None, ALU.add))
            m1 = s_ve.n
            @block.scalar
            def _(sc):
                preload(sc, set_lnexp)
                sc.wait_ge(s_ve.h, m1)
                sc.activation(sp_scr2[:, :], dt_raw[:, :], AF.Abs)
                sc.activation(sp_scr2[:, :], sp_scr2[:, :], AF.Exp, scale=-1.0)
                s_ac.inc(sc.activation(sp_scr2[:, :], sp_scr2[:, :], AF.Ln, bias=1.0))
            m2 = s_ac.n
            @block.vector
            def _(v):
                v.wait_ge(s_ac.h, m2)
                v.tensor_scalar(dt_raw[:, :], dt_raw[:, :], 0.0, None, ALU.max)
                v.tensor_tensor(dt_sp[:, :], dt_raw[:, :], sp_scr2[:, :], ALU.add)
                v.memset(zer[:, :], 0.0)
                s_ve.inc(v.tensor_scalar(a_dt[:, :], dt_sp[:, :], acol[:, :], None, ALU.mult))
            adt_ve = s_ve.n
            tap("t_dtsp", dt_sp[:, :])

            @block.sync
            def _(sync):
                sync.wait_ge(s_gs2[0].h, gate_spilled[0])
                sync.wait_ge(s_gs2[1].h, gate_spilled[1])
                sync.wait_ge(s_pe.h, last_yn_pe.get(0, 0))
                gate_pref[0] = s_gpre.inc(sync.dma_start(out=gate_ch[0], in_=gate_sp[0:128, :]), 16)

            for c in range(4):
                cb = 3 + c * 128

                @block.vector
                def _(v):
                    if c == 0:
                        v.wait_ge(s_ve.h, adt_ve)
                        if first:
                            v.memset(s_state[:, :], 0.0)
                    else:
                        v.wait_ge(s_sp.h, s_sp.n)
                    v.tensor_tensor_scan(acsT[:, :], a_dt[:, c * 128:(c + 1) * 128],
                                         zer, 0.0, ALU.add, ALU.add)
                    v.tensor_copy(hilo[:, 0:128], acsT[:, :])
                    v.tensor_tensor(hilo[:, 128:256], acsT[:, :], hilo[:, 0:128], ALU.subtract)
                    s_ve.inc(v.tensor_scalar(wdecT[:, :], acsT[:, :], acsT[:, 127:128], -1.0,
                                             ALU.subtract, ALU.mult))
                ach1 = s_ve.n
                @block.scalar
                def _(sc):
                    sc.wait_ge(s_ve.h, ach1)
                    s_ac.inc(sc.activation(wdecT[:, :], wdecT[:, :], AF.Exp))
                wdec_ac = s_ac.n
                @block.vector
                def _(v):
                    v.wait_ge(s_ac.h, wdec_ac)
                    s_ve.inc(v.tensor_tensor(dtwT[:, :], wdecT[:, :],
                                             dt_sp[:, c * 128:(c + 1) * 128], ALU.mult))
                dtw_ve = s_ve.n

                @block.gpsimd
                def _(g):
                    g.wait_ge(s_ve.h, ach1)
                    g.wait_ge(s_pe.h, prev_flat_pe)
                    s_sp.inc(g.dma_start(out=flat[0:1, :], in_=hilo[:, 0:128]), 16)
                    s_sp.inc(g.dma_start(out=flat[1:2, :], in_=hilo[:, 128:256]), 16)
                flat_sp = s_sp.n

                @block.tensor
                def _(t):
                    if c == 0:
                        t.wait_ge(s_ac.h, conv_ac)
                    t.wait_ge(s_ve.h, dtw_ve)
                    t.wait_ge(s_ac.h, prev_prep_ac)
                    t.wait_ge(s_ve.h, prev_ynT_ve)
                    t.transpose(p5[:, 0:64], dt_sp[:, c * 128:(c + 1) * 128], identf[0:64, 0:64])
                    t.transpose(p5[:, 64:128], acsT[:, :], identf[0:64, 0:64])
                    t.transpose(p5[:, 128:192], dtwT[:, :], identf[0:64, 0:64])
                    t.matmul(p5[:, 192:320], convin[:, 32 * CW + cb:32 * CW + cb + 128],
                             convin[:, 33 * CW + cb:33 * CW + cb + 128], start=True, stop=True)
                    t.wait_ge(s_sp.h, flat_sp)
                    fv = flat.rearrange("c (h t) -> c h t", h=NH)[:, :, 127]
                    t.matmul(p5[:, 320:384], ones2[:, :], fv, start=True, stop=True)
                    s_pe.inc(t.transpose(pb16[:, 0:128],
                                         convin[:, 32 * CW + cb:32 * CW + cb + 128], identb[:, :]))
                prep_pe = s_pe.n

                @block.vector
                def _(v):
                    v.wait_ge(s_pe.h, prep_pe)
                    v.tensor_copy(stok[:, :], p5[:, 0:192])
                    v.tensor_tensor(gm[:, :], p5[:, 192:320], triu01[:, :], ALU.mult)
                    s_ve.inc(v.tensor_copy(b_tok[:, :], pb16[:, 0:128]))
                prep_ve = s_ve.n

                @block.scalar
                def _(sc):
                    sc.wait_ge(s_ve.h, prep_ve)
                    sc.activation(expa[:, :], stok[:, 64:128], AF.Exp)
                    s_ac.inc(sc.activation(decr[:, :], p5[:, 320:384], AF.Exp))
                prep_ac = s_ac.n
                prev_prep_ac = prep_ac

                pe_b_xm = s_pe.n
                ve_b_xm = s_ve.n
                @block.tensor
                def _(t):
                    t.wait_ge(s_ve.h, prep_ve)
                    t.wait_ge(s_ac.h, prep_ac)
                    for r in range(8):
                        if r >= 2:
                            t.wait_ge(s_ve.h, ve_b_xm + r - 1)
                        dstp = pxb[r % 2]
                        ti = None
                        for kk in range(4):
                            j = r * 4 + kk
                            ti = t.transpose(dstp[:, kk * 128:(kk + 1) * 128],
                                             convin[:, j * CW + cb:j * CW + cb + 128],
                                             identb[:, :])
                        s_pe.inc(ti)
                @block.vector
                def _(v):
                    for r in range(8):
                        v.wait_ge(s_pe.h, pe_b_xm + r + 1)
                        s_ve.inc(v.tensor_copy(x_m[:, r * 512:(r + 1) * 512], pxb[r % 2]))
                assert s_ve.n == ve_b_xm + 8
                xm_ve = s_ve.n

                if c == 0:
                    tap("t_xmtok", x_m)
                    tap("t_acs", acsT[:, :])

                for g8 in range(8):
                    @block.vector
                    def _(v):
                        if c == 0 and g8 == 0:
                            v.wait_ge(s_gs2[0].h, gate_spilled[0])
                            v.wait_ge(s_gs2[1].h, gate_spilled[1])
                        if g8 >= 2:
                            v.wait_ge(s_pe.h, s16_cons[g8 - 2])
                        s16_m[g8] = s_ve.inc(v.tensor_copy(s16g[g8 % 2],
                                                           s_state[:, g8 * 512:(g8 + 1) * 512]))

                    @block.gpsimd
                    def _(g):
                        g.wait_ge(s_ve.h, xm_ve)
                        if c == 0 and g8 == 0:
                            g.wait_ge(s_gs2[0].h, gate_spilled[0])
                            g.wait_ge(s_gs2[1].h, gate_spilled[1])
                        if g8 >= 2:
                            g.wait_ge(s_pe.h, mt_cons[(g8 - 2) * 8 + 7])
                        dv = dh_rep[:, g8 * 8:(g8 + 1) * 8].unsqueeze(2).broadcast_to((128, 8, 64))
                        s_gp.inc(g.tensor_tensor(r3(dres[g8 % 2], 64),
                                                 r3(x_m[:, g8 * 512:(g8 + 1) * 512], 64), dv, ALU.mult))
                        g.wait_ge(s_ve.h, prep_ve)
                        wv = r3(stok[:, 128:192], 1)[:, g8 * 8:(g8 + 1) * 8, :].broadcast_to((128, 8, 64))
                        s_gp.inc(g.tensor_tensor(r3(xcw[g8 % 2], 64),
                                                 r3(x_m[:, g8 * 512:(g8 + 1) * 512], 64), wv, ALU.mult))
                        g.wait_ge(s_ac.h, prep_ac)
                        if g8 >= 1:
                            g.wait_ge(s_ve.h, supd[g8 - 1])
                        dcv = decr[:, g8 * 8:(g8 + 1) * 8].unsqueeze(2).broadcast_to((128, 8, 64))
                        s_gp.inc(g.tensor_tensor(r3(sdec[g8 % 2], 64),
                                                 r3(s_state[:, g8 * 512:(g8 + 1) * 512], 64), dcv, ALU.mult))
                    pool_gp = s_gp.n

                    @block.tensor
                    def _(t):
                        if g8 == 0:
                            t.wait_ge(s_sp.h, flat_sp)
                        if g8 >= 1:
                            t.wait_ge(s_ve.h, dneg_m[g8 - 1])
                        t.matmul(pA[:, 0:512], ones2[:, :], flat[:, g8 * 1024:g8 * 1024 + 512],
                                 start=True, stop=True)
                        s_pe.inc(t.matmul(pA[:, 512:1024], ones2[:, :],
                                          flat[:, g8 * 1024 + 512:(g8 + 1) * 1024],
                                          start=True, stop=True))
                    arep_pe = s_pe.n
                    if g8 == 7:
                        prev_flat_pe = arep_pe

                    @block.vector
                    def _(v):
                        v.wait_ge(s_pe.h, arep_pe)
                        if g8 >= 2:
                            v.wait_ge(s_ac.h, e_m[g8 - 2])
                        av = r3(stok[:, 64:128], 1)[:, g8 * 8:(g8 + 1) * 8, :].broadcast_to((128, 8, 128))
                        v.tensor_tensor(r3(dnegb[g8 % 2], 128), r3(pA[:, :], 128), av, ALU.subtract)
                        dneg_m[g8] = s_ve.inc(v.tensor_scalar(dnegb[g8 % 2], dnegb[g8 % 2],
                                                              0.0, None, ALU.min))

                    @block.scalar
                    def _(sc):
                        sc.wait_ge(s_ve.h, dneg_m[g8])
                        if g8 >= 2:
                            sc.wait_ge(s_ve.h, mt_cons_ve[(g8 - 2) * 8 + 7])
                        e_m[g8] = s_ac.inc(sc.activation(e_sb[g8 % 2], dnegb[g8 % 2], AF.Exp))

                    @block.tensor
                    def _(t):
                        t.wait_ge(s_ve.h, s16_m[g8])
                        if g8 >= 1:
                            t.wait_ge(s_ve.h, yevac[g8 - 1])
                        t.matmul(p2[:, :], convin[:, 33 * CW + cb:33 * CW + cb + 128],
                                 s16g[g8 % 2], start=True, stop=True)
                        t.wait_ge(s_gp.h, pool_gp)
                        s_pe.inc(t.matmul(p1[:, :], identb[:, :], dres[g8 % 2],
                                          start=True, stop=False, skip_group_check=True))
                    yoff_pe = s_pe.n
                    s16_cons[g8] = yoff_pe

                    for hh in range(8):
                        u = g8 * 8 + hh
                        @block.vector
                        def _(v):
                            v.wait_ge(s_ac.h, e_m[g8])
                            if u >= 2:
                                v.wait_ge(s_pe.h, mt_cons[u - 2])
                            mt_cons_ve[u] = s_ve.inc(
                                v.scalar_tensor_tensor(mt[u % 2], gm[:, :],
                                                       stok[:, u:u + 1],
                                                       e_sb[g8 % 2][:, hh * 128:(hh + 1) * 128],
                                                       ALU.mult, ALU.mult))
                        @block.tensor
                        def _(t):
                            t.wait_ge(s_ve.h, mt_cons_ve[u])
                            mt_cons[u] = s_pe.inc(
                                t.matmul(p1[:, hh * 64:(hh + 1) * 64], mt[u % 2],
                                         x_m[:, u * 64:(u + 1) * 64],
                                         start=False, stop=(hh == 7), skip_group_check=True))
                    ydiag_pe = s_pe.n

                    @block.tensor
                    def _(t):
                        if g8 >= 1:
                            t.wait_ge(s_ve.h, supd[g8 - 1])
                        s_pe.inc(t.matmul(p3[:, :], b_tok[:, :], xcw[g8 % 2],
                                          start=True, stop=True))
                    st_pe = s_pe.n

                    @block.vector
                    def _(v):
                        v.wait_ge(s_pe.h, yoff_pe)
                        if g8 == 0:
                            v.wait_ge(s_ac.h, prep_ac)
                        ev = expa[:, g8 * 8:(g8 + 1) * 8].unsqueeze(2).broadcast_to((128, 8, 64))
                        v.tensor_tensor(r3(tmpg, 64), r3(p2[:, :], 64), ev, ALU.mult)
                        v.wait_ge(s_pe.h, ydiag_pe)
                        yevac[g8] = s_ve.inc(
                            v.tensor_tensor(y_tok[:, g8 * 512:(g8 + 1) * 512],
                                            tmpg, p1[:, :], ALU.add))
                        v.wait_ge(s_pe.h, st_pe)
                        supd[g8] = s_ve.inc(
                            v.tensor_tensor(s_state[:, g8 * 512:(g8 + 1) * 512],
                                            sdec[g8 % 2], p3[:, :], ALU.add))
                y_ve = s_ve.n
                if c == 0:
                    tap("t_y", y_tok)
                if c == 3:
                    tap("t_S", s_state[:, :])

                if c < 3:
                    @block.sync
                    def _(sync):
                        sync.wait_ge(s_ve.h, y_ve)
                        sync.wait_ge(s_pe.h, last_yn_pe.get((c + 1) % 2, 0))
                        gate_pref[c + 1] = s_gpre.inc(
                            sync.dma_start(out=gate_ch[(c + 1) % 2],
                                           in_=gate_sp[(c + 1) * 128:(c + 2) * 128, :]), 16)

                @block.vector
                def _(v):
                    v.wait_ge(s_gpre.h, gate_pref[c])
                    v.wait_ge(s_ve.h, y_ve)
                    s_ve.inc(v.tensor_tensor(y_tok, y_tok, gate_ch[c % 2], ALU.mult))
                    for s in range(8):
                        v.bn_stats(bn6[:, s * 6:(s + 1) * 6], y_tok[:, s * 512:(s + 1) * 512])
                    v.tensor_copy(spacr[:, 0:32], identf[:, 0:32])
                    v.tensor_copy(spacr[:, 0:32], identf[:, 32:64])
                    v.tensor_copy(spacr[:, 0:32], identf[:, 64:96])
                    v.bn_aggr(tiny[:, 0:2], bn6[:, 0:48])
                    v.tensor_copy(spacr[:, 0:32], identf[:, 0:32])
                    v.tensor_copy(spacr[:, 0:32], identf[:, 32:64])
                    v.tensor_copy(spacr[:, 0:32], identf[:, 64:96])
                    v.scalar_tensor_tensor(tiny[:, 3:4], tiny[:, 0:1], tiny[:, 0:1],
                                           tiny[:, 1:2], ALU.mult, ALU.add)
                    v.tensor_copy(spacr[:, 0:32], identf[:, 0:32])
                    v.tensor_copy(spacr[:, 0:32], identf[:, 32:64])
                    s_ve.inc(v.tensor_copy(spacr[:, 0:32], identf[:, 64:96]))
                rms_ve = s_ve.n
                @block.scalar
                def _(sc):
                    sc.wait_ge(s_ve.h, rms_ve)
                    sc.activation(tiny[:, 5:6], tiny[:, 3:4], AF.Ln, bias=epsc[:, :])
                    s_ac.inc(sc.activation(tiny[:, 5:6], tiny[:, 5:6], AF.Exp, scale=-0.5))
                rms_ac = s_ac.n
                if c == 3 and "t_dbg_rms" in taps:
                    @block.gpsimd
                    def _(g):
                        g.wait_ge(s_ac.h, rms_ac)
                        s_sp.inc(g.dma_start(out=taps["t_dbg_rms"][:, 0:8], in_=tiny[:, :]), 16)
                        s_sp.inc(g.dma_start(out=taps["t_dbg_rms"][:, 8:56], in_=bn6[:, 0:48]), 16)
                    @block.vector
                    def _(v):
                        v.wait_ge(s_sp.h, s_sp.n)
                @block.vector
                def _(v):
                    v.wait_ge(s_ac.h, rms_ac)
                    if c > 0:
                        v.wait_ge(s_ynsp.h, ynsp[c - 1])
                    s_ve.inc(v.scalar_tensor_tensor(gate_ch[c % 2], y_tok, tiny[:, 5:6],
                                                    nw_rep[:, :], ALU.mult, ALU.mult))
                yn_ve = s_ve.n
                if c == 3:
                    tap("t_yn", gate_ch[c % 2])

                pe_b_yn = s_pe.n
                ve_b_yn = s_ve.n
                @block.tensor
                def _(t):
                    t.wait_ge(s_ve.h, yn_ve)
                    for r in range(8):
                        if r >= 2:
                            t.wait_ge(s_ve.h, ve_b_yn + r - 1)
                        dstp = pxb[r % 2]
                        ti = None
                        for kk in range(4):
                            k = r * 4 + kk
                            ti = t.transpose(dstp[:, kk * 128:(kk + 1) * 128],
                                             gate_ch[c % 2][:, k * 128:(k + 1) * 128], identb[:, :])
                        s_pe.inc(ti)
                @block.vector
                def _(v):
                    for r in range(8):
                        v.wait_ge(s_pe.h, pe_b_yn + r + 1)
                        s_ve.inc(v.tensor_copy(ynT_ch[:, r * 512:(r + 1) * 512], pxb[r % 2]))
                ynT_ve = s_ve.n
                prev_ynT_ve = ynT_ve
                last_yn_pe[c % 2] = s_pe.n
                @block.sync
                def _(sync):
                    sync.wait_ge(s_ve.h, ynT_ve)
                    ynsp[c] = s_ynsp.inc(
                        sync.dma_start(out=ynT_sp3[c:c + 1], in_=r3(ynT_ch, 128)), 16)
            ynT_spld = s_ynsp.n
            tap("t_ynT", ynT_sp3)

            if p == 0 and wout_cast:
                emit_cast(wout_cast)
                wout_cast = []

            if STAGE < 4:
                _store(block, (s_ve, s_ac), s_out, x_res, out, bi, r0)
                continue

            pbanks = [pA[:, 0:512], pA[:, 512:1024], p1[:, :], p2[:, :], p3[:, :],
                      p4[:, :], p5[:, :], pb16[:, :].bitcast(F32)]
            opv = {}
            opv_ac = {}
            for half in range(2):
                pe_b_o = s_pe.n
                ymark = {}
                wmark_o = {}
                @block.sync
                def _(sync):
                    sync.wait_ge(s_ynsp.h, ynT_spld)
                    sync.wait_ge(s_pe.h, s_pe.n)
                    for kt in range(32):
                        sl = kt % 4
                        if kt >= 4:
                            sync.wait_ge(s_pe.h, pe_b_o + kt - 3)
                        ymark[kt] = s_y4[sl].inc(sync.dma_start(
                            out=ynstr[sl],
                            in_=ynT_sp3.rearrange("c p k t -> p c k t")[:, :, kt:kt + 1, :]), 16)
                ldq = block.scalar if p == 0 else block.gpsimd

                @ldq
                def _(sc):
                    sc.wait_ge(s_pe.h, pe_after_x if half == 0 else pe_b_o)
                    for kt in range(32):
                        sl = kt % 3
                        if p == 0:
                            wait_cast(sc, ("wo", kt))
                        if kt >= 3:
                            sc.wait_ge(s_pe.h, pe_b_o + kt - 2)
                        src = wtile3(woutT16, kt * 128, 1, half * 1024, 1024)
                        wmark_o[kt] = s_w3[sl].inc(
                            sc.dma_start(out=r3(warena[sl][:, 0:1024], 1024), in_=src), 16)

                @block.tensor
                def _(t):
                    t.wait_ge(s_ve.h, s_ve.n)
                    t.wait_ge(s_ac.h, s_ac.n)
                    for kt in range(32):
                        t.wait_ge(s_y4[kt % 4].h, ymark[kt])
                        t.wait_ge(s_w3[kt % 3].h, wmark_o[kt])
                        ti = None
                        for tt in range(NT):
                            for db in range(2):
                                ti = t.matmul(pbanks[tt * 2 + db],
                                              ynstr[kt % 4][:, tt * 128:(tt + 1) * 128],
                                              warena[kt % 3][:, db * 512:(db + 1) * 512],
                                              start=(kt == 0), stop=(kt == 31),
                                              skip_group_check=True)
                        s_pe.inc(ti)
                op_mm = s_pe.n

                for b in range(8):
                    tt, db = b // 2, b % 2
                    dst = attn_tok[:, tt * D + half * 1024 + db * 512:
                                   tt * D + half * 1024 + (db + 1) * 512]
                    if b % 2 == 0:
                        @block.vector
                        def _(v):
                            v.wait_ge(s_pe.h, op_mm)
                            s_ve.inc(v.tensor_copy(dst, pbanks[b]))
                    else:
                        @block.scalar
                        def _(sc):
                            sc.wait_ge(s_pe.h, op_mm)
                            s_ac.inc(sc.activation(dst, pbanks[b], AF.Copy))
                opv[half] = s_ve.n
                opv_ac[half] = s_ac.n
            tap("t_attn", attn_tok)

            if p == 0 and late_cast:
                emit_cast(late_cast)
                late_cast = []

            @block.vector
            def _(v):
                v.wait_ge(s_ac.h, opv_ac[1])
            ln_stats(attn_tok)
            ln2_ac = s_ac.n
            @block.vector
            def _(v):
                v.wait_ge(s_ac.h, ln2_ac)
                for i in range(NT):
                    v.tensor_scalar(attn_tok[:, i * D:(i + 1) * D], attn_tok[:, i * D:(i + 1) * D],
                                    mv[:, 2 * i:2 * i + 1], rstd[:, i:i + 1],
                                    ALU.subtract, ALU.mult)
                    s_ve.inc(v.tensor_tensor(x_res[:, i * D:(i + 1) * D],
                                             x_res[:, i * D:(i + 1) * D],
                                             attn_tok[:, i * D:(i + 1) * D], ALU.add))
            tap("t_x1", x_res[:, :])

            if STAGE < 5:
                _store(block, (s_ve,), s_out, x_res, out, bi, r0)
                continue

            @block.vector
            def _(v):
                v.wait_ge(s_ld.h, ynT_spld)
            ln_stats(x_res)
            @block.tensor
            def _(t):
                t.wait_ge(s_ve.h, s_ve.n)
            ln_apply_transpose(x_res)
            xn2_ve = s_ve.n

            pe_b_f = s_pe.n
            wmark_f = {}
            @block.sync
            def _(sc):
                sc.wait_ge(s_pe.h, pe_b_f)
                for j in range(NJT):
                    sl0 = (2 * j) % 3
                    sl1 = (2 * j + 1) % 3
                    if p == 0:
                        wait_cast(sc, ("fg", j))
                        wait_cast(sc, ("fu", j))
                    if j >= 2:
                        sc.wait_ge(s_pe.h, pe_b_f + j - 1)
                    wmark_f[2 * j] = s_w3[sl0].inc(sc.dma_start(out=warena[sl0][:, :],
                                                   in_=wgs16[j:j + 1]), 16)
                    if j >= 1:
                        sc.wait_ge(s_pe.h, pe_b_f + j)
                    wmark_f[2 * j + 1] = s_w3[sl1].inc(sc.dma_start(out=warena[sl1][:, :],
                                                       in_=wus16[j:j + 1]), 16)

            for j in range(NJT):
                pg = p1 if j % 2 == 0 else p2
                pu = p3 if j % 2 == 0 else p4
                @block.tensor
                def _(t):
                    t.wait_ge(s_w3[(2 * j) % 3].h, wmark_f[2 * j])
                    t.wait_ge(s_w3[(2 * j + 1) % 3].h, wmark_f[2 * j + 1])
                    if j < 2:
                        t.wait_ge(s_ve.h, xn2_ve)
                    else:
                        t.wait_ge(s_ve.h, h_m[j - 2])
                        t.wait_ge(s_ac.h, sg_m[j - 2])
                    for k in range(KD):
                        t.matmul(pg[:, :], warena[(2 * j) % 3][:, k * 128:(k + 1) * 128],
                                 xnT[:, k * TP:(k + 1) * TP], start=(k == 0), stop=(k == KD - 1))
                    ti = None
                    for k in range(KD):
                        ti = t.matmul(pu[:, :], warena[(2 * j + 1) % 3][:, k * 128:(k + 1) * 128],
                                      xnT[:, k * TP:(k + 1) * TP], start=(k == 0), stop=(k == KD - 1))
                    s_pe.inc(ti)
                fmm = s_pe.n
                if j == 0 and "t_dbg_g" in taps:
                    @block.vector
                    def _(v):
                        v.wait_ge(s_pe.h, fmm)
                        s_ve.inc(v.tensor_copy(attn_tok[:, 0:512], pg[:, :]))
                        s_ve.inc(v.tensor_copy(attn_tok[:, 512:1024], pu[:, :]))
                    dbgv = s_ve.n
                    @block.gpsimd
                    def _(sync):
                        sync.wait_ge(s_ve.h, dbgv)
                        s_sp.inc(sync.dma_start(out=taps["t_dbg_g"], in_=attn_tok[:, 0:512]), 16)
                        s_sp.inc(sync.dma_start(out=taps["t_dbg_u"], in_=attn_tok[:, 512:1024]), 16)
                    @block.scalar
                    def _(sc):
                        sc.wait_ge(s_sp.h, s_sp.n)
                @block.scalar
                def _(sc):
                    if j == 0:
                        preload(sc, set_silu)
                    sc.wait_ge(s_pe.h, fmm)
                    sg_m[j] = s_ac.inc(sc.activation(sg_j[j % 2], pg[:, :], AF.Silu,
                                                     scale=gs_c[:, j:j + 1]))
                if j == 0 and "t_dbg_sg" in taps:
                    @block.gpsimd
                    def _(sync):
                        sync.wait_ge(s_ac.h, sg_m[0])
                        s_sp.inc(sync.dma_start(out=taps["t_dbg_sg"], in_=sg_j[0]), 16)
                    @block.vector
                    def _(v):
                        v.wait_ge(s_sp.h, s_sp.n)
                @block.vector
                def _(v):
                    v.wait_ge(s_ac.h, sg_m[j])
                    h_m[j] = s_ve.inc(
                        v.scalar_tensor_tensor(hT(j), pu[:, :],
                                               us_c[:, j:j + 1], sg_j[j % 2],
                                               ALU.mult, ALU.mult))
            h_ve = s_ve.n
            if "t_h" in taps:
                tap("t_h", convin_b[:, 0:32 * TP])

            pe_b_d = s_pe.n
            sg_last = sg_m[NJT - 1]
            wmark_d = {}
            @block.sync
            def _(sync):
                sync.wait_ge(s_pe.h, pe_b_d)
                s_dsr.inc(sync.dma_start(out=ds_rep, in_=dsr_in), 16)
            dsr_m = s_dsr.n
            @block.scalar
            def _(sc):
                preload(sc, set_lnexp)
                sc.wait_ge(s_pe.h, pe_b_d)
                for idx in range(2 * NJT):
                    half, k = idx // NJT, idx % NJT
                    sl = idx % 3
                    if p == 0:
                        wait_cast(sc, ("wd", k))
                    if idx >= 3:
                        sc.wait_ge(s_pe.h, pe_b_d + idx - 2)
                    src = wtile3(wdT16, k * 128, 1, half * 1024, 1024)
                    wmark_d[idx] = s_w3[sl].inc(
                        sc.dma_start(out=r3(warena[sl][:, 0:1024], 1024), in_=src), 16)

            dnv = {}
            for half in range(2):
                @block.tensor
                def _(t):
                    if half == 0:
                        t.wait_ge(s_ve.h, h_ve)
                        t.wait_ge(s_ac.h, sg_last)
                    else:
                        t.wait_ge(s_ve.h, dnv[0])
                    for k in range(NJT):
                        idx = half * NJT + k
                        t.wait_ge(s_w3[idx % 3].h, wmark_d[idx])
                        ti = None
                        for tt in range(NT):
                            for db in range(2):
                                ti = t.matmul(pbanks[tt * 2 + db],
                                              hT(k)[:, tt * 128:(tt + 1) * 128],
                                              warena[idx % 3][:, db * 512:(db + 1) * 512],
                                              start=(k == 0), stop=(k == NJT - 1),
                                              skip_group_check=True)
                        s_pe.inc(ti)
                dmm = s_pe.n
                @block.vector
                def _(v):
                    v.wait_ge(s_pe.h, dmm)
                    if half == 0:
                        v.wait_ge(s_dsr.h, dsr_m)
                    for b in range(8):
                        tt, db = b // 2, b % 2
                        dst = attn_tok[:, tt * D + half * 1024 + db * 512:
                                       tt * D + half * 1024 + (db + 1) * 512]
                        s_ve.inc(v.tensor_tensor(dst, pbanks[b],
                                                 ds_rep[:, half * 1024 + db * 512:
                                                        half * 1024 + (db + 1) * 512],
                                                 ALU.mult))
                dnv[half] = s_ve.n

            ln_stats(attn_tok)
            ln4_ac = s_ac.n
            @block.vector
            def _(v):
                v.wait_ge(s_ac.h, ln4_ac)
                for i in range(NT):
                    v.tensor_scalar(attn_tok[:, i * D:(i + 1) * D], attn_tok[:, i * D:(i + 1) * D],
                                    mv[:, 2 * i:2 * i + 1], rstd[:, i:i + 1],
                                    ALU.subtract, ALU.mult)
                    s_ve.inc(v.tensor_tensor(x_res[:, i * D:(i + 1) * D],
                                             x_res[:, i * D:(i + 1) * D],
                                             attn_tok[:, i * D:(i + 1) * D], ALU.add))
            _store(block, (s_ve,), s_out, x_res, out, bi, r0)

        @block.sync
        def _(sync):
            sync.wait_ge(s_out.h, s_out.n)
            sync.wait_ge(s_ld.h, s_ld.n)
            sync.wait_ge(s_sp.h, s_sp.n)
            sync.wait_ge(s_gpre.h, s_gpre.n)
            sync.wait_ge(s_ynsp.h, s_ynsp.n)
            sync.wait_ge(s_dsr.h, s_dsr.n)
            for sm in s_w3 + s_y4 + s_gs2 + s_cl2 + s_cw2:
                sync.wait_ge(sm.h, sm.n)

    return nc, taps


def _store(block, sems, s_out, x_res, out, bi, r0):
    @block.sync
    def _(sync):
        for sm in sems:
            sync.wait_ge(sm.h, sm.n)
        for i in range(NT):
            s_out.inc(sync.dma_start(out=out[bi, r0 + i * 128:r0 + (i + 1) * 128, :],
                                     in_=x_res[:, i * D:(i + 1) * D]), 16)


def _layout_inputs(inputs):
    f32 = np.float32
    asC = np.ascontiguousarray
    w = {}
    Wi = np.asarray(inputs["L0_attn_in_proj_w"], f32)
    w["win_g"] = asC(Wi[0:DIN].reshape(8, 512, 4, 4, 128)
                     .transpose(0, 2, 4, 3, 1).reshape(32, 128, 2048))
    wx = Wi[DIN:DIN + CONVD].reshape(NCT, 128, KD, 128).transpose(0, 3, 2, 1)
    wdt = Wi[DIN + CONVD:].reshape(64, KD, 128).transpose(2, 1, 0).reshape(1, 128, KD * 64)
    wxa = np.zeros((35, 128, KD * 128), f32)
    wxa[0:NCT] = wx.reshape(NCT, 128, KD * 128)
    wxa[NCT, :, 0:KD * 64] = wdt[0]
    w["win_x"] = asC(wxa)
    w["woutT"] = asC(np.asarray(inputs["L0_attn_out_proj_w"], f32).T)
    w["wg_s"] = asC(np.asarray(inputs["L0_ffn_gate_w"], f32)
                    .reshape(NJT, 128, KD, 128).transpose(0, 3, 2, 1).reshape(NJT, 128, KD * 128))
    w["wu_s"] = asC(np.asarray(inputs["L0_ffn_up_w"], f32)
                    .reshape(NJT, 128, KD, 128).transpose(0, 3, 2, 1).reshape(NJT, 128, KD * 128))
    w["wdT"] = asC(np.asarray(inputs["L0_ffn_down_w"], f32).T)
    w["convw"] = asC(np.asarray(inputs["L0_attn_conv_w"], f32).reshape(NCT, 128, 4)
                     .transpose(1, 0, 2).reshape(128, NCT * 4))
    w["convb"] = asC(np.asarray(inputs["L0_attn_conv_b"], f32).reshape(NCT, 128).T)
    w["dtb"] = asC(np.asarray(inputs["L0_attn_dt_bias"], f32).reshape(NH, 1))
    w["alog"] = asC(np.asarray(inputs["L0_attn_A_log"], f32).reshape(NH, 1))
    w["dh_rep"] = asC(np.broadcast_to(np.asarray(inputs["L0_attn_D"], f32)[None, :], (128, NH)))
    w["nw_rep"] = asC(np.broadcast_to(np.asarray(inputs["L0_attn_norm_w"], f32)[None, :], (128, DIN)))
    w["gs"] = asC(np.asarray(inputs["L0_ffn_gate_s"], f32).reshape(NJT, 128).T)
    w["us"] = asC(np.asarray(inputs["L0_ffn_up_s"], f32).reshape(NJT, 128).T)
    w["ds_rep"] = asC(np.broadcast_to(np.asarray(inputs["L0_ffn_down_s"], f32)[None, :],
                                      (128, D)))
    return w


def kernel(**inputs):
    x = np.asarray(inputs["L0_attn_hidden_states"], np.float32)
    w = _layout_inputs(inputs)
    nc, _ = build_program()
    in_maps = []
    for c in range(8):
        m = dict(w)
        m["x"] = np.ascontiguousarray(x[2 * c:2 * c + 2])
        in_maps.append(m)
    res = run_bass_kernel_spmd(nc, in_maps, core_ids=list(range(8)))
    return np.concatenate([r["out"] for r in res.results], axis=0)
